# version 16
# baseline (speedup 1.0000x reference)
import numpy as np
import ml_dtypes
import concourse.bass as bass
import concourse.mybir as mybir
from concourse.bass_utils import run_bass_kernel_spmd
from contextlib import ExitStack

F32 = mybir.dt.float32
BF16 = mybir.dt.bfloat16
FP8 = mybir.dt.float8e4
AF = mybir.ActivationFunctionType
ALU = mybir.AluOpType
AX = mybir.AxisListType
BF = ml_dtypes.bfloat16

MAGIC = float(np.float32(1.5 * 2 ** 23))
T = 2048
D = 2048
NH = 32
NKV = 8
HD = 64
WIN = 1024
INNER = 8192
NCH = D // 128
EPS_LN = 1e-5
EPS_QK = 1e-6
EPS_RMS = 1e-6
EPS_Q = 1e-5
WMAX = WIN + 128


def _slopes():
    r = 2.0 ** (-8.0 / NH)
    return np.array([r ** i for i in range(NH)], dtype=np.float64)


def _alibi_strips():
    sl32 = np.float32([np.float64(2.0 ** (-8.0 / NH)) ** i for i in range(NH)])
    p = np.arange(128, dtype=np.int64)[:, None]
    u = np.arange(WMAX, dtype=np.int64)[None, :]
    d = p - u
    strips = np.empty((NH, 128, WMAX), np.float32)
    for h in range(NH):
        val = sl32[h] * d.astype(np.float32)
        strips[h] = np.where((d <= 0) & (d >= -WIN), val, np.float32(-1e30))
    return strips.reshape(NH * 128, WMAX)


def _host_consts():
    return dict(
        c_ident=np.eye(128, dtype=BF),
        c_ones=np.ones((128, 128), dtype=BF),
        c_zeros=np.zeros((1, 512), dtype=ml_dtypes.float8_e4m3),
        alibi_strips=_alibi_strips(),
        c_onesf=np.ones((65, 128), dtype=np.float32),
        c_identf=np.eye(128, dtype=np.float32),
        c_onesc=np.ones((128, 1), dtype=np.float32),
        c_onescb=np.ones((128, 1), dtype=BF),
    )


class Sem:
    def __init__(self, handle):
        self.h = handle
        self.n = 0

    def inc(self, instr, k=1):
        self.n += k
        instr.then_inc(self.h, k)
        return self.n


class DmaQ:
    def __init__(self, emitter, mk_sems, n=4):
        self.sems = mk_sems(n)
        self.emit = emitter
        self.i = 0

    def dma(self, fn):
        s = self.sems[self.i % len(self.sems)]
        self.i += 1
        if s.n > 0:
            self.emit(lambda e, s=s, v=s.n: e.wait_ge(s.h, v))
        ins = self.emit(fn)
        s.inc(ins, 16)
        return s, s.n

    def wait_all(self, emitter):
        for s in self.sems:
            if s.n > 0:
                emitter(lambda e, s=s, v=s.n: e.wait_ge(s.h, v))


class InsProxy:
    __slots__ = ("incs",)

    def __init__(self):
        self.incs = []

    def then_inc(self, sem, k):
        self.incs.append((sem, k))
        return self


class Bld:
    ENGS = ("tensor", "vector", "scalar", "gpsimd", "sync")

    def __init__(self, block):
        self.block = block
        self.pend = {e: [] for e in self.ENGS}

    def _emit(self, eng, fn):
        p = InsProxy()
        self.pend[eng].append((fn, p))
        return p

    def flush(self):
        for eng in self.ENGS:
            lst = self.pend[eng]
            if not lst:
                continue
            self.pend[eng] = []

            def run(e, lst=lst):
                for fn, p in lst:
                    ins = fn(e)
                    if p.incs:
                        assert ins is not None
                        for sem, k in p.incs:
                            ins.then_inc(sem, k)
                return None
            getattr(self.block, eng)(run)

    def t(self, fn):
        return self._emit("tensor", fn)

    def v(self, fn):
        return self._emit("vector", fn)

    def s(self, fn):
        return self._emit("scalar", fn)

    def g(self, fn):
        return self._emit("gpsimd", fn)

    def y(self, fn):
        return self._emit("sync", fn)


def build_program(stages=("A", "B", "C"), debug=()):
    nc = bass.Bass("TRN2", target_bir_lowering=False)
    dbg = set(debug)
    stages = set(stages)

    def dram_in(name, shape, dt=F32):
        return nc.dram_tensor(name, list(shape), dt, kind="ExternalInput").ap()

    d_x = dram_in("x_tok", (T, D))
    d_wq = dram_in("wqT_blk", (16 * D, 128))
    d_wkv = dram_in("wkvT_blk", (8 * D, 128))
    d_wo = dram_in("woT", (D, D))
    d_w1T = dram_in("w1T", (D, INNER))
    d_w2T = dram_in("w2T", (INNER, D))
    d_qnw = dram_in("qnw_col", (128, NCH))
    d_knw = dram_in("knw_col", (128, 4))
    d_b1 = dram_in("b1_row", (1, INNER))
    d_b2 = dram_in("b2_row", (1, D))
    d_ident = dram_in("c_ident", (128, 128), BF16)
    d_ones = dram_in("c_ones", (128, 128), BF16)
    d_zeros = dram_in("c_zeros", (1, 512), FP8)
    d_astrip = dram_in("alibi_strips", (NH * 128, WMAX))
    d_onesf = dram_in("c_onesf", (65, 128))
    d_onescb = dram_in("c_onescb", (128, 1), BF16)
    d_identf = dram_in("c_identf", (128, 128))
    d_onesc = dram_in("c_onesc", (128, 1))

    d_out = nc.dram_tensor("out", [T, D], F32, kind="ExternalOutput").ap()
    d_dbg = {}

    def dbg_out(name, shape, dt=F32):
        d_dbg[name] = nc.dram_tensor("dbg_" + name, list(shape), dt,
                                     kind="ExternalOutput").ap()
        return d_dbg[name]

    d_w1s = nc.dram_tensor("scr_w1s", [8 * NCH * 128, 1024], FP8, kind="Internal").ap()
    d_w2s = nc.dram_tensor("scr_w2s", [64 * 128, D], FP8, kind="Internal").ap()
    d_tmp2 = nc.dram_tensor("scr_tmp2", [T, D], F32, kind="Internal").ap()

    ctx = ExitStack()
    E = ctx.enter_context
    scopes = {k: ExitStack() for k in ("SA", "SQKV", "SB", "SBC", "SCW", "SC")}

    def sb(name, shape, dt=BF16, scope=None, side="left"):
        cm = nc.sbuf_tensor(name, list(shape), dt, side=side)
        if scope is None:
            return E(cm)
        return scopes[scope].enter_context(cm)

    s_id = None
    s_ones = None
    s_onescb = sb("s_onescb", (128, 1))
    s_zeros = sb("s_zeros", (1, 512), FP8)
    s_onesf = sb("s_onesf", (65, 128), F32)
    s_mgc = sb("s_mgc", (128, 1), F32)
    s_mgcn = sb("s_mgcn", (128, 1), F32)


    s_lnm = sb("s_lnm", (128, NCH), F32)
    s_lnr = sb("s_lnr", (128, NCH), F32)
    s_lnv = sb("s_lnv", (128, NCH), F32)
    s_mx1 = sb("s_mx1", (128, NCH), F32)

    s_st = sb("s_st", (128, 32), F32)
    s_pw = sb("s_pw", (128, 8), F32)
    s_onesc = sb("s_onesc", (128, 1), F32)

    s_idfb = sb("s_idfb", (128, 128), F32, "SQKV")
    s_qF = sb("s_qF", (128, NCH, T), BF16, "SQKV")
    s_kF = [sb(f"s_kF{g}", (128, T), BF16, "SQKV") for g in range(NKV)]
    s_vF = sb("s_vF", (128, NCH, NKV * (HD + 1)), BF16, "SQKV")
    s_xnT = sb("s_xnT", (128, NCH, 1024), BF16, "SA")
    s_xt = [sb(f"s_xt{i}", (128, D), F32, "SA") for i in range(2)]
    s_xn = sb("s_xn", (128, D), F32, "SA")
    s_xnb = sb("s_xnb", (128, D), BF16, "SA")
    s_wblk = [sb(f"s_wblk{i}", (128, NCH, 128), BF16, "SA") for i in range(2)]
    s_id = sb("s_id", (128, 128), BF16, "SA")
    s_ones = sb("s_ones", (128, 128), BF16, "SA")
    s_qnw8 = sb("s_qnw8", (128, NCH), F32, "SA")
    s_knw = sb("s_knw", (128, 4), F32, "SA")
    s_bn = sb("s_bn", (128, 24), F32, "SA")
    s_rsq = [sb(f"s_rsq{i}", (128, 512), F32, "SA") for i in range(2)]
    s_ra = sb("s_ra", (128, 512), F32, "SA")
    s_rb = sb("s_rb", (128, 512), F32, "SA")
    s_sq = sb("s_sq", (128, 512), BF16, "SA")

    sems = {}

    def sem(name):
        sems[name] = Sem(E(nc.semaphore(name)))
        return sems[name]

    m_T = sem("m_T")
    m_V = sem("m_V")
    m_S = sem("m_S")
    m_pv = sem("m_pv")
    m_G = sem("m_G")

    blk = E(nc.Block())
    b = Bld(blk)

    def mk_sems(prefix):
        return lambda n: [Sem(E(nc.semaphore(f"{prefix}{i}"))) for i in range(n)]
    Qy = DmaQ(b.y, mk_sems("qy"), 6)
    Qg = DmaQ(b.g, mk_sems("qg"), 6)
    Qs = DmaQ(b.s, mk_sems("qs"), 4)
    Qst = DmaQ(b.s, mk_sems("qst"), 4)

    pctx = [None]
    pool_id = [0]

    def psum_pool(n_f32, n_bf16=0):
        b.flush()
        if pctx[0] is not None:
            pctx[0].close()
            tn, vn, sn = m_T.n, m_V.n, m_S.n
            for emit in (b.t, b.v, b.s, b.g, b.y):
                emit(lambda e, tn=tn: e.wait_ge(m_T.h, tn))
                emit(lambda e, vn=vn: e.wait_ge(m_V.h, vn))
                emit(lambda e, sn=sn: e.wait_ge(m_S.h, sn))
        pctx[0] = ExitStack()
        pid = pool_id[0]
        pool_id[0] += 1
        if isinstance(n_f32, int):
            shapes = [512] * n_f32
        else:
            shapes = n_f32
        pf = [pctx[0].enter_context(nc.psum_tensor(f"pf{pid}_{i}", [128, w], F32))
              for i, w in enumerate(shapes)]
        pbf = [pctx[0].enter_context(nc.psum_tensor(f"pb{pid}_{i}", [128, 1024], BF16))
               for i in range(n_bf16)]
        return pf, pbf

    def flush_banks(pf):
        ins = None
        for pp in pf:
            ins = b.t(lambda t, pp=pp: t.matmul(
                pp[:], s_zeros[0:1, 0:128], s_zeros[0:1, 0:512], start=True, stop=True))
        return ins

    for dst, src in ((s_id, d_ident), (s_ones, d_ones), (s_zeros, d_zeros),
                     (s_knw, d_knw), (s_qnw8, d_qnw), (s_onesf, d_onesf),
                     (s_onesc, d_onesc), (s_onescb, d_onescb),
                     (s_idfb, d_identf)):
        Qy.dma(lambda y, dst=dst, src=src: y.dma_start(out=dst[:], in_=src))
    for em in (b.v, b.t, b.s, b.g):
        Qy.wait_all(em)
    b.v(lambda v: v.tensor_scalar_mul(out=s_qnw8[:], in0=s_qnw8[:], scalar1=0.125))
    b.v(lambda v: v.drain())
    M52 = float(1.5 * 2 ** 52)
    b.v(lambda v: v.memset(s_mgc[:], 1.0))
    b.v(lambda v: v.drain())
    b.v(lambda v: v.tensor_scalar(out=s_mgc[:], in0=s_mgc[:], scalar1=float(2.0 ** -30),
                                  scalar2=None, op0=ALU.add))
    b.v(lambda v: v.drain())
    b.v(lambda v: v.tensor_scalar(out=s_mgc[:], in0=s_mgc[:], scalar1=1.0,
                                  scalar2=None, op0=ALU.subtract))
    b.v(lambda v: v.drain())
    b.v(lambda v: v.tensor_scalar(out=s_mgc[:], in0=s_mgc[:], scalar1=float(2.0 ** 30),
                                  scalar2=None, op0=ALU.mult))
    b.v(lambda v: v.drain())
    b.v(lambda v: v.tensor_scalar(out=s_mgc[:], in0=s_mgc[:], scalar1=float(M52 - MAGIC),
                                  scalar2=MAGIC, op0=ALU.mult, op1=ALU.add))
    b.v(lambda v: v.drain())
    b.v(lambda v: v.tensor_scalar_mul(out=s_mgcn[:], in0=s_mgc[:], scalar1=-1.0))
    b.v(lambda v: v.drain())
    for gg_ in range(NKV):
        b.v(lambda v, gg_=gg_: v.memset(s_vF[:, :, gg_ * 65 + 64:gg_ * 65 + 65], 1.0))
        b.v(lambda v: v.drain())
    m_V.inc(b.v(lambda v: v.memset(s_pw[:], 0.0)))

    prep_state = {}
    prep_rd = {}
    prep_ld = {}
    s_prep = [None, None]
    s_prep8 = [None, None]

    def prep_load(k):
        pi = s_prep[k % 2]
        pi8 = s_prep8[k % 2]
        if k < 64:
            dd, iq = k // 4, k % 4
            src = d_w1T[dd * 128:(dd + 1) * 128, iq * 2048:(iq + 1) * 2048]
            dsts = [(d_w1s[((2 * iq + j) * NCH + dd) * 128:((2 * iq + j) * NCH + dd) * 128 + 128, :],
                     pi8[:, j * 1024:(j + 1) * 1024]) for j in range(2)]
            acc = s_pw[:, 0:1]
        else:
            ic = k - 64
            src = d_w2T[ic * 128:(ic + 1) * 128, :]
            dsts = [(d_w2s[ic * 128:(ic + 1) * 128, :], pi8[:])]
            acc = s_pw[:, 1:2]
        if k >= 1:
            sg_p, va_p = prep_rd[k - 1]
            b.g(lambda g, sg_p=sg_p: g.wait_ge(m_S.h, sg_p))
            if va_p:
                b.g(lambda g, va_p=va_p: g.wait_ge(m_V.h, va_p))
        ls, lv = Qg.dma(lambda g, pi=pi, src=src: g.dma_start(out=pi[:], in_=src))
        prep_ld[k] = (ls, lv, dsts, acc)

    def prep_fin(k):
        pi = s_prep[k % 2]
        pi8 = s_prep8[k % 2]
        ls, lv, dsts, acc = prep_ld[k]
        b.v(lambda v, ls=ls, lv=lv: v.wait_ge(ls.h, lv))
        b.v(lambda v, pi=pi: v.tensor_reduce(out=s_st[:, 20:21], in_=pi[:], axis=AX.X,
                                             op=ALU.add, apply_absolute_value=True))
        b.v(lambda v: v.drain())
        m_V.inc(b.v(lambda v, acc=acc: v.tensor_add(out=acc, in0=acc,
                                                    in1=s_st[:, 20:21])))
        va_n = m_V.n
        b.s(lambda s, ls=ls, lv=lv: s.wait_ge(ls.h, lv))
        if k >= 1:
            for gs, gv in prep_state[k - 1]:
                b.s(lambda s, gs=gs, gv=gv: s.wait_ge(gs.h, gv))
        m_S.inc(b.s(lambda s, pi=pi, pi8=pi8: s.activation(out=pi8[:], in_=pi[:],
                                                           func=AF.Sign)))
        sg_n = m_S.n
        prep_rd[k] = (sg_n, va_n)
        b.s(lambda s, sg_n=sg_n: s.wait_ge(m_S.h, sg_n))
        prep_state[k] = [Qs.dma(lambda s, d=ddst, sr=ssrc: s.dma_start(out=d, in_=sr))
                         for ddst, ssrc in dsts]

    def prep_finalize():
        pf, _ = psum_pool(1)
        vn = m_V.n
        b.t(lambda t, vn=vn: t.wait_ge(m_V.h, vn))
        m_T.inc(b.t(lambda t: t.matmul(pf[0][0:1, 0:2], s_onesc[:, 0:1],
                                       s_pw[:, 0:2], start=True, stop=True)))
        tn = m_T.n
        b.v(lambda v, tn=tn: v.wait_ge(m_T.h, tn))
        b.v(lambda v: v.drain())
        m_V.inc(b.v(lambda v: v.tensor_copy(out=s_st[0:1, 22:24],
                                            in_=pf[0][0:1, 0:2])))
        vn2 = m_V.n
        b.t(lambda t, vn2=vn2: t.wait_ge(m_V.h, vn2))
        m_T.inc(b.t(lambda t: t.matmul(pf[0][:, 0:2], s_onesf[0:1, :],
                                       s_st[0:1, 22:24], start=True, stop=True)))
        tn = m_T.n
        b.v(lambda v, tn=tn: v.wait_ge(m_T.h, tn))
        b.v(lambda v: v.tensor_scalar_mul(out=s_pw[:, 2:3], in0=pf[0][:, 0:1],
                                          scalar1=1.0 / (INNER * D)))
        b.v(lambda v: v.drain())
        m_V.inc(b.v(lambda v: v.tensor_scalar_mul(out=s_pw[:, 3:4], in0=pf[0][:, 1:2],
                                                  scalar1=1.0 / (INNER * D))))

    a_done = {}

    def phase_A_half(half, p_trb):
        for ti in range(8):
            tt = half * 8 + ti
            buf = s_xt[ti % 2]
            if tt >= 2:
                gate = a_done[f"xn{tt - 2}"]
                b.y(lambda y, gate=gate: y.wait_ge(m_V.h, gate))
            lsx, lvx = Qy.dma(lambda y, buf=buf, tt=tt: y.dma_start(
                out=buf[:], in_=d_x[tt * 128:(tt + 1) * 128, :]))
            b.v(lambda v, lsx=lsx, lvx=lvx: v.wait_ge(lsx.h, lvx))
            if tt >= 1:
                gate = a_done[f"ev{tt - 1}_1"]
                b.v(lambda v, gate=gate: v.wait_ge(m_V.h, gate))
            for q4 in range(4):
                b.v(lambda v, buf=buf, q4=q4: v.bn_stats(
                    out=s_bn[:, q4 * 6:(q4 + 1) * 6],
                    in_=buf[:, q4 * 512:(q4 + 1) * 512]))
            b.v(lambda v: v.drain())
            b.v(lambda v: v.bn_aggr(out=s_st[:, 0:2], in_=s_bn[:]))
            b.v(lambda v: v.drain())
            b.v(lambda v, tt=tt: v.tensor_copy(out=s_lnm[:, tt:tt + 1], in_=s_st[:, 0:1]))
            b.v(lambda v, tt=tt: v.tensor_copy(out=s_lnv[:, tt:tt + 1], in_=s_st[:, 1:2]))
            b.v(lambda v: v.tensor_scalar(out=s_st[:, 2:3], in0=s_st[:, 1:2],
                                          scalar1=EPS_LN, scalar2=None, op0=ALU.add))
            b.v(lambda v: v.drain())
            m_V.inc(b.v(lambda v: v.reciprocal(out=s_st[:, 4:5], in_=s_st[:, 2:3])))
            v1 = m_V.n
            b.s(lambda s, v1=v1: s.wait_ge(m_V.h, v1))
            m_S.inc(b.s(lambda s: s.activation(out=s_st[:, 5:6], in_=s_st[:, 4:5],
                                               func=AF.Sqrt)))
            s1 = m_S.n
            b.v(lambda v, s1=s1: v.wait_ge(m_S.h, s1))
            b.v(lambda v: v.tensor_mul(out=s_st[:, 7:8], in0=s_st[:, 5:6],
                                       in1=s_st[:, 5:6]))
            b.v(lambda v: v.drain())
            b.v(lambda v: v.tensor_mul(out=s_st[:, 8:9], in0=s_st[:, 7:8],
                                       in1=s_st[:, 2:3]))
            b.v(lambda v: v.drain())
            b.v(lambda v: v.tensor_scalar(out=s_st[:, 9:10], in0=s_st[:, 8:9],
                                          scalar1=-0.5, scalar2=1.5,
                                          op0=ALU.mult, op1=ALU.add))
            b.v(lambda v: v.drain())
            b.v(lambda v, tt=tt: v.tensor_mul(out=s_lnr[:, tt:tt + 1],
                                              in0=s_st[:, 5:6], in1=s_st[:, 9:10]))
            b.v(lambda v: v.drain())
            b.v(lambda v, buf=buf, tt=tt: v.tensor_scalar(
                out=s_xn[:], in0=buf[:], scalar1=s_lnm[:, tt:tt + 1],
                scalar2=s_lnr[:, tt:tt + 1], op0=ALU.subtract, op1=ALU.mult))
            b.v(lambda v: v.drain())
            b.v(lambda v, tt=tt: v.reduce_max(out=s_mx1[:, tt:tt + 1], in_=s_xn[:],
                                              axis=AX.X, apply_absolute_value=True))
            m_V.inc(b.v(lambda v: v.tensor_copy(out=s_xnb[:], in_=s_xn[:])))
            a_done[f"xn{tt}"] = m_V.n
            v2 = m_V.n
            b.t(lambda t, v2=v2: t.wait_ge(m_V.h, v2))
            for bb in range(2):
                gate = a_done.get(f"ev{tt}_{bb - 1}", a_done.get(f"ev{tt - 1}_1", 0))
                b.t(lambda t, gate=gate: t.wait_ge(m_V.h, gate))
                ins = None
                for kk in range(8):
                    c = bb * 8 + kk
                    ins = b.t(lambda t, kk=kk, c=c, p_trb=p_trb: t.transpose(
                        p_trb[:, kk * 128:(kk + 1) * 128],
                        s_xnb[:, c * 128:(c + 1) * 128], s_id[:]))
                m_T.inc(ins)
                tb = m_T.n
                b.v(lambda v, tb=tb: v.wait_ge(m_T.h, tb))
                b.v(lambda v: v.drain())
                m_V.inc(b.v(lambda v, bb=bb, ti=ti, p_trb=p_trb: v.tensor_copy(
                    out=s_xnT[:, bb * 8:(bb + 1) * 8, ti * 128:(ti + 1) * 128],
                    in_=p_trb[:].rearrange("p (c t) -> p c t", c=8))))
                a_done[f"ev{tt}_{bb}"] = m_V.n

    def rsq512(psq, nfeat, eps, which):
        dst = s_rsq[which]
        b.v(lambda v, psq=psq, dst=dst, nfeat=nfeat, eps=eps: v.tensor_scalar(
            out=dst[:], in0=psq[:], scalar1=1.0 / nfeat, scalar2=eps,
            op0=ALU.mult, op1=ALU.add))
        b.v(lambda v: v.drain())
        m_V.inc(b.v(lambda v, dst=dst: v.reciprocal(out=s_ra[:], in_=dst[:])))
        vv = m_V.n
        b.s(lambda s, vv=vv: s.wait_ge(m_V.h, vv))
        m_S.inc(b.s(lambda s: s.activation(out=s_rb[:], in_=s_ra[:], func=AF.Sqrt)))
        ss = m_S.n
        b.v(lambda v, ss=ss: v.wait_ge(m_S.h, ss))
        b.v(lambda v: v.tensor_mul(out=s_ra[:], in0=s_rb[:], in1=s_rb[:]))
        b.v(lambda v: v.drain())
        b.v(lambda v, dst=dst: v.tensor_mul(out=s_ra[:], in0=s_ra[:], in1=dst[:]))
        b.v(lambda v: v.drain())
        b.v(lambda v: v.tensor_scalar(out=s_ra[:], in0=s_ra[:], scalar1=-0.5,
                                      scalar2=1.5, op0=ALU.mult, op1=ALU.add))
        b.v(lambda v: v.drain())
        m_V.inc(b.v(lambda v, dst=dst: v.tensor_mul(out=dst[:], in0=s_rb[:],
                                                    in1=s_ra[:])))
        b.v(lambda v: v.drain())
        return dst

    def proj_half(half, pf):
        p_pr = pf[0:2]
        p_sq_q = pf[2:4]
        p_sq_k = pf[4:6]
        ev = {}
        prev_mm = m_T.n
        prev_ms = m_S.n
        prev_mv = m_V.n

        def otile(kind, ot, idx):
            src = (d_wkv if kind == "k" else d_wq)[ot * D:(ot + 1) * D, :]
            wb = s_wblk[idx % 2]
            gate = ev.get(f"mm_{idx - 2}", prev_mm)
            b.g(lambda g, gate=gate: g.wait_ge(m_T.h, gate))
            lsw, lvw = Qg.dma(lambda g, wb=wb, src=src: g.dma_start(
                out=wb[:], in_=src.rearrange("(c p) o -> p c o", p=128)))
            b.t(lambda t, lsw=lsw, lvw=lvw: t.wait_ge(lsw.h, lvw))
            for tb in range(2):
                pp = p_pr[tb]
                gate = ev.get(f"ev_{tb}", prev_mv)
                b.t(lambda t, gate=gate: t.wait_ge(m_V.h, gate))
                ins = None
                for dd in range(NCH):
                    ins = b.t(lambda t, dd=dd, tb=tb, wb=wb, pp=pp: t.matmul(
                        pp[:], wb[:, dd, :], s_xnT[:, dd, tb * 512:(tb + 1) * 512],
                        start=(dd == 0), stop=(dd == NCH - 1)))
                m_T.inc(ins)
                tn = m_T.n
                b.s(lambda s, tn=tn: s.wait_ge(m_T.h, tn))
                m_S.inc(b.s(lambda s, pp=pp: s.activation(out=s_sq[:], in_=pp[:],
                                                          func=AF.Square)))
                sqn = m_S.n
                b.t(lambda t, sqn=sqn: t.wait_ge(m_S.h, sqn))
                psq = (p_sq_k if kind == "k" else p_sq_q)[tb]
                m_T.inc(b.t(lambda t, psq=psq, first=(ot == 0): t.matmul(
                    psq[:], s_ones[:, :], s_sq[:], start=first, stop=False)))
                col = slice(half * 1024 + tb * 512, half * 1024 + (tb + 1) * 512)
                sqn2 = m_S.n
                b.v(lambda v, sqn2=sqn2: v.wait_ge(m_S.h, sqn2))
                b.v(lambda v: v.drain())
                if kind == "q":
                    m_V.inc(b.v(lambda v, ot=ot, col=col, pp=pp: v.tensor_copy(
                        out=s_qF[:, ot, col], in_=pp[:])))
                else:
                    m_V.inc(b.v(lambda v, ot=ot, col=col, pp=pp: v.tensor_copy(
                        out=s_kF[2 * ot][0:64, col], in_=pp[0:64, :])))
                    m_V.inc(b.v(lambda v, ot=ot, col=col, pp=pp: v.tensor_copy(
                        out=s_kF[2 * ot + 1][64:128, col], in_=pp[64:128, :])))
                ev[f"ev_{tb}"] = m_V.n
            ev[f"mm_{idx}"] = m_T.n

        for ot in range(4):
            otile("k", ot, ot)
        for ot in range(16):
            otile("q", ot, 4 + ot)

        for psq in (p_sq_q[0], p_sq_q[1], p_sq_k[0], p_sq_k[1]):
            m_T.inc(b.t(lambda t, psq=psq: t.matmul(
                psq[:], s_zeros[0:1, 0:128], s_zeros[0:1, 0:512],
                start=False, stop=True)))
        tn_sq = m_T.n
        b.v(lambda v, tn_sq=tn_sq: v.wait_ge(m_T.h, tn_sq))

        for tb in range(2):
            col = slice(half * 1024 + tb * 512, half * 1024 + (tb + 1) * 512)
            rq = rsq512(p_sq_q[tb], D, EPS_QK, 0)
            for j in range(NCH):
                m_V.inc(b.v(lambda v, j=j, col=col, rq=rq: v.scalar_tensor_tensor(
                    out=s_qF[:, j, col], in0=s_qF[:, j, col],
                    scalar=s_qnw8[:, j:j + 1], in1=rq[:],
                    op0=ALU.mult, op1=ALU.mult)))
            rk = rsq512(p_sq_k[tb], 512, EPS_QK, 1)
            for kk in range(4):
                m_V.inc(b.v(lambda v, kk=kk, col=col, rk=rk: v.scalar_tensor_tensor(
                    out=s_kF[2 * kk][0:64, col], in0=s_kF[2 * kk][0:64, col],
                    scalar=s_knw[0:64, kk:kk + 1], in1=rk[0:64, :],
                    op0=ALU.mult, op1=ALU.mult)))
                m_V.inc(b.v(lambda v, kk=kk, col=col, rk=rk: v.scalar_tensor_tensor(
                    out=s_kF[2 * kk + 1][64:128, col],
                    in0=s_kF[2 * kk + 1][64:128, col],
                    scalar=s_knw[64:128, kk:kk + 1], in1=rk[64:128, :],
                    op0=ALU.mult, op1=ALU.mult)))
        vn = m_V.n
        b.s(lambda s, vn=vn: s.wait_ge(m_V.h, vn))
        for g in range(NKV):
            col = slice(half * 1024, half * 1024 + 1024)
            if g % 2 == 0:
                ks_, kv_ = Qs.dma(lambda s, g=g, col=col: s.dma_start(
                    out=s_kF[g][64:128, col], in_=s_kF[g][0:64, col]))
            else:
                ks_, kv_ = Qs.dma(lambda s, g=g, col=col: s.dma_start(
                    out=s_kF[g][0:64, col], in_=s_kF[g][64:128, col]))
            kdup_waits.append((ks_, kv_))

        for vot in range(4):
            wb = s_wblk[vot % 2]
            gate = ev.get(f"vmm_{vot - 2}", ev.get("mm_19", prev_mm))
            b.g(lambda g, gate=gate: g.wait_ge(m_T.h, gate))
            src = d_wkv[(4 + vot) * D:(5 + vot) * D, :]
            lsw, lvw = Qg.dma(lambda g, wb=wb, src=src: g.dma_start(
                out=wb[:], in_=src.rearrange("(c p) o -> p c o", p=128)))
            b.t(lambda t, lsw=lsw, lvw=lvw: t.wait_ge(lsw.h, lvw))
            for ti in range(8):
                tt = half * 8 + ti
                pp = p_pr[ti % 2]
                if ti >= 2:
                    gate = ev[f"vev_{vot}_{ti - 2}"]
                elif vot >= 1:
                    gate = ev[f"vev_{vot - 1}_{ti + 6}"]
                else:
                    gate = ev.get("ev_1", prev_mv)
                b.t(lambda t, gate=gate: t.wait_ge(m_V.h, gate))
                ins = None
                for dd in range(NCH):
                    ins = b.t(lambda t, dd=dd, ti=ti, pp=pp, wb=wb: t.matmul(
                        pp[:, 0:128], s_xnT[:, dd, ti * 128:(ti + 1) * 128],
                        wb[:, dd, :], start=(dd == 0), stop=(dd == NCH - 1)))
                m_T.inc(ins)
                tn = m_T.n
                b.v(lambda v, tn=tn: v.wait_ge(m_T.h, tn))
                b.v(lambda v: v.drain())
                b.v(lambda v, tt=tt, vot=vot, pp=pp: v.tensor_copy(
                    out=s_vF[:, tt, (2 * vot) * 65:(2 * vot) * 65 + 64],
                    in_=pp[:, 0:64]))
                m_V.inc(b.v(lambda v, tt=tt, vot=vot, pp=pp: v.tensor_copy(
                    out=s_vF[:, tt, (2 * vot + 1) * 65:(2 * vot + 1) * 65 + 64],
                    in_=pp[:, 64:128])))
                ev[f"vev_{vot}_{ti}"] = m_V.n
            ev[f"vmm_{vot}"] = m_T.n

    kdup_waits = []
    s_attnT = None


    def phase_B():
        pf, _ = psum_pool([512, 512, 512, 2048])
        ring = pf[0:3]
        pvo = pf[3]
        for em in (b.t, b.v, b.s):
            for ks_, kv_ in kdup_waits:
                em(lambda e, ks_=ks_, kv_=kv_: e.wait_ge(ks_.h, kv_))
        b.flush()
        scopes["SA"].close()
        NPT = 3
        s_PT = sb("s_PT", (128, NPT, WMAX), BF16, "SB")
        s_rsD = sb("s_rsD", (128, 1024), F32, "SB")
        s_ab = [sb(f"s_ab{i}", (128, WMAX), F32, "SB") for i in range(2)]
        nonlocal s_prep, s_prep8, s_attnT
        s_prep = [sb("s_prep0", (128, D), BF16, "SB")] * 2
        s_prep8 = [sb("s_prep8", (128, D), FP8, "SB")] * 2
        s_attnT = sb("s_attnT", (128, NCH, T), BF16, "SBC", side="right")
        ab_ld = {}
        ab_va = {}
        ab_ld[0] = Qy.dma(lambda y: y.dma_start(out=s_ab[0][:],
                                                in_=d_astrip[0:128, :]))

        slot_gate = [0, 0, 0]
        pt_gate = [0, 0, 0]
        prev_evac = 0
        sl = _slopes()

        for h in range(NH):
            g = h >> 2
            hb = h & 1
            j = h >> 1
            rows = slice(hb * 64, hb * 64 + 64)
            if h + 1 < NH:
                gt, gv = ab_va.get(h - 1, (0, 0))
                b.y(lambda y, gt=gt: y.wait_ge(m_T.h, gt))
                b.y(lambda y, gv=gv: y.wait_ge(m_V.h, gv))
                ab_ld[h + 1] = Qy.dma(lambda y, h=h: y.dma_start(
                    out=s_ab[(h + 1) % 2][:],
                    in_=d_astrip[(h + 1) * 128:(h + 2) * 128, :]))
            prep_load(4 * h)
            b.t(lambda t, pe=prev_evac: t.wait_ge(m_V.h, pe))
            for bb in range(4):
                ins = b.t(lambda t, bb=bb: t.matmul(
                    pvo[:, bb * 512:(bb + 1) * 512], s_zeros[0:1, 0:128],
                    s_zeros[0:1, 0:512], start=True, stop=True))
            m_T.inc(ins)
            flush_n = m_T.n
            sub_ctr = [0]
            chunk_exp = {}

            def emit_chunk(c, g=g, hb=hb, j=j, rows=rows, h=h):
                Wc = min(WMAX, T - c * 128)
                nsub = (Wc + 511) // 512
                slot = c % NPT
                exp_gate_pv = pt_gate[slot]
                exp_first = True
                for sref in range(nsub):
                    a0 = sref * 512
                    b0 = min(a0 + 512, Wc)
                    ws = b0 - a0
                    si = sub_ctr[0]
                    sub_ctr[0] += 1
                    rslot = ring[si % 3]
                    gate_s = slot_gate[si % 3]
                    b.t(lambda t, gate_s=gate_s: t.wait_ge(m_S.h, gate_s))
                    if c == 0 and sref == 0:
                        ls_ab, lv_ab = ab_ld[h]
                        b.t(lambda t, ls_ab=ls_ab, lv_ab=lv_ab: t.wait_ge(
                            ls_ab.h, lv_ab))
                    on_T = False
                    if on_T:
                        b.t(lambda t, rslot=rslot, c=c, a0=a0, ws=ws, g=g,
                            j=j, rows=rows: t.matmul(
                            rslot[:, 0:ws], s_kF[g][rows, c * 128:(c + 1) * 128],
                            s_qF[rows, j, c * 128 + a0: c * 128 + a0 + ws],
                            start=True, stop=False))
                        m_T.inc(b.t(lambda t, rslot=rslot, h=h, a0=a0,
                                    ws=ws: t.matmul(
                            rslot[:, 0:ws], s_idfb[:],
                            s_ab[h % 2][:, a0:a0 + ws],
                            start=False, stop=True)))
                        mm_n = m_T.n
                        b.s(lambda sc, mm_n=mm_n: sc.wait_ge(m_T.h, mm_n))
                    else:
                        m_T.inc(b.t(lambda t, rslot=rslot, c=c, a0=a0, ws=ws,
                                    g=g, j=j, rows=rows: t.matmul(
                            rslot[:, 0:ws], s_kF[g][rows, c * 128:(c + 1) * 128],
                            s_qF[rows, j, c * 128 + a0: c * 128 + a0 + ws],
                            start=True, stop=True)))
                        mm_n = m_T.n
                        b.v(lambda v, mm_n=mm_n: v.wait_ge(m_T.h, mm_n))
                        if c == 0:
                            ls_ab, lv_ab = ab_ld[h]
                            b.v(lambda v, ls_ab=ls_ab, lv_ab=lv_ab: v.wait_ge(
                                ls_ab.h, lv_ab))
                        m_V.inc(b.v(lambda v, rslot=rslot, h=h, a0=a0, ws=ws:
                                    v.tensor_add(
                            out=rslot[:, 0:ws], in0=rslot[:, 0:ws],
                            in1=s_ab[h % 2][:, a0:a0 + ws])))
                        va_n = m_V.n
                        b.s(lambda sc, va_n=va_n: sc.wait_ge(m_V.h, va_n))
                    if exp_first:
                        b.s(lambda sc, gp=exp_gate_pv: sc.wait_ge(m_pv.h, gp))
                        exp_first = False
                    ins = b.s(lambda sc, rslot=rslot, slot=slot, a0=a0, b0=b0,
                              ws=ws: sc.activation(
                        out=s_PT[:, slot, a0:b0],
                        in_=rslot[:, 0:ws], func=AF.Exp, scale=1.0))
                    m_S.inc(ins)
                    slot_gate[si % 3] = m_S.n
                chunk_exp[c] = m_S.n

            def emit_pv(c, g=g, hb=hb):
                Wc = min(WMAX, T - c * 128)
                slot = c % NPT
                en = chunk_exp[c]
                b.t(lambda t, en=en: t.wait_ge(m_S.h, en))
                gc0 = c * 128
                gc1 = c * 128 + Wc
                pieces = []
                p0 = gc0
                while p0 < gc1:
                    p1 = min((p0 // 512 + 1) * 512, gc1)
                    pieces.append((p0, p1))
                    p0 = p1
                ins = None
                for (p0, p1) in pieces:
                    w0 = p0 - gc0
                    ins = b.t(lambda t, w0=w0, p0=p0, p1=p1, c=c, g=g,
                              slot=slot: t.matmul(
                        pvo[0:65, p0:p1],
                        s_vF[:, c, g * 65:(g + 1) * 65],
                        s_PT[:, slot, w0:w0 + (p1 - p0)],
                        start=False, stop=False,
                        skip_group_check=True))
                m_pv.inc(ins)
                pt_gate[slot] = m_pv.n

            emit_chunk(0)
            for c in range(16):
                if c + 1 < 16:
                    emit_chunk(c + 1)
                emit_pv(c)
                if c in (3, 6, 9, 12):
                    kk = c // 3 - 1
                    prep_fin(4 * h + kk)
                    if kk < 3:
                        prep_load(4 * h + kk + 1)
            ab_va[h] = (m_T.n, m_V.n)
            pvn = m_pv.n
            drow = slice(64, 65)
            orow = slice(0, 64)
            orow_at = slice(hb * 64, hb * 64 + 64)
            b.s(lambda sc, pvn=pvn: sc.wait_ge(m_pv.h, pvn))
            b.s(lambda sc: sc.drain())
            ln_done = {}
            m_S.inc(b.s(lambda sc, drow=drow: sc.activation(
                out=s_rsD[64:65, 0:1024], in_=pvo[drow, 0:1024], func=AF.Ln)))
            ln_done[0] = m_S.n
            tp = (64, 0)
            exp_done = {}
            stt_done = {}

            def bcast(bb, orow=orow, tp=tp, drow=drow):
                rslot = ring[bb % 3]
                gate_s = slot_gate[bb % 3]
                if bb == 2 and 1 not in ln_done:
                    btn = m_T.n
                    b.s(lambda sc, btn=btn: sc.wait_ge(m_T.h, btn))
                    m_S.inc(b.s(lambda sc, drow=drow: sc.activation(
                        out=s_rsD[64:65, 0:1024], in_=pvo[drow, 1024:2048], func=AF.Ln)))
                    ln_done[1] = m_S.n
                b.t(lambda t, lnn=ln_done[min(bb // 2, 1)]: t.wait_ge(m_S.h, lnn))
                b.t(lambda t, gate_s=gate_s: t.wait_ge(m_S.h, gate_s))
                if bb == 3:
                    b.t(lambda t, ex=exp_done[0]: t.wait_ge(m_S.h, ex))
                m_T.inc(b.t(lambda t, rslot=rslot, bb=bb, orow=orow, tp=tp: t.matmul(
                    rslot[orow, :], s_onesf[64:65, 0:64],
                    s_rsD[64:65, (bb % 2) * 512 + 0:(bb % 2) * 512 + 512],
                    start=True, stop=True, tile_position=tp)))
                return m_T.n

            def expev(bb, bcn, orow=orow, j=j):
                rslot = ring[bb % 3]
                b.s(lambda sc, bcn=bcn: sc.wait_ge(m_T.h, bcn))
                if bb >= 2:
                    b.s(lambda sc, g2=stt_done[bb - 2]: sc.wait_ge(m_V.h, g2))
                m_S.inc(b.s(lambda sc, rslot=rslot, bb=bb, orow=orow: sc.activation(
                    out=s_rsD[orow, (bb % 2) * 512:(bb % 2 + 1) * 512],
                    in_=rslot[orow, :], func=AF.Exp, scale=-1.0)))
                slot_gate[bb % 3] = m_S.n
                exp_done[bb] = m_S.n
                rdn = m_S.n
                b.v(lambda v, rdn=rdn: v.wait_ge(m_S.h, rdn))
                m_V.inc(b.v(lambda v, bb=bb, j=j, orow=orow,
                            orow_at=orow_at: v.scalar_tensor_tensor(
                    out=s_attnT[orow_at, j, bb * 512:(bb + 1) * 512],
                    in0=pvo[orow, bb * 512:(bb + 1) * 512], scalar=1.0,
                    in1=s_rsD[orow, (bb % 2) * 512:(bb % 2 + 1) * 512],
                    op0=ALU.mult, op1=ALU.mult)))
                stt_done[bb] = m_V.n

            bc = [bcast(0), bcast(1), bcast(2)]
            expev(0, bc[0])
            expev(1, bc[1])
            bc.append(bcast(3))
            expev(2, bc[2])
            expev(3, bc[3])
            prev_evac = m_V.n
        b.t(lambda t, pe=prev_evac: t.wait_ge(m_V.h, pe))
        for bb in range(4):
            m_T.inc(b.t(lambda t, bb=bb: t.matmul(
                pvo[:, bb * 512:(bb + 1) * 512], s_zeros[0:1, 0:128],
                s_zeros[0:1, 0:512], start=True, stop=True)))
        prep_finalize()
        return prev_evac


    C1 = float(np.sqrt(D))
    C2 = float(np.sqrt(INNER))

    def phase_C():
        pf, _ = psum_pool(8)
        for em in (b.y, b.g, b.s):
            for Q in (Qs, Qg, Qy):
                Q.wait_all(em)
        b.flush()
        scopes["SB"].close()
        scopes["SQKV"].close()

        s_woF = sb("s_woF", (128, NCH, D), BF16, "SCW", side="right")
        s_b2B = sb("s_b2B", (128, D), F32, "SCW")
        s_tm = [sb(f"s_tm{i}", (128, D), F32, "SCW") for i in range(2)]
        s_xw = [sb(f"s_xw{i}", (128, D), F32, "SCW") for i in range(2)]

        ev_b = 0
        row_t = 0
        for i in range(D // 512):
            b.y(lambda y, g=row_t: y.wait_ge(m_T.h, g))
            ls_, lv_ = Qy.dma(lambda y, i=i: y.dma_start(
                out=s_tm[0][0:1, 0:512], in_=d_b2[0:1, i * 512:(i + 1) * 512]))
            pp = pf[i % 2]
            b.t(lambda t, ls_=ls_, lv_=lv_: t.wait_ge(ls_.h, lv_))
            if ev_b:
                b.t(lambda t, ev_b=ev_b: t.wait_ge(m_V.h, ev_b))
            m_T.inc(b.t(lambda t, pp=pp: t.matmul(
                pp[:], s_onesf[0:1, :], s_tm[0][0:1, 0:512], start=True, stop=True)))
            row_t = m_T.n
            b.v(lambda v, tn=row_t: v.wait_ge(m_T.h, tn))
            b.v(lambda v: v.drain())
            m_V.inc(b.v(lambda v, pp=pp, i=i: v.tensor_copy(
                out=s_b2B[:, i * 512:(i + 1) * 512], in_=pp[:])))
            ev_b = m_V.n
        b2_v = m_V.n
        b2_t = m_T.n

        wo_ld = []
        for dH in range(NCH):
            wo_ld.append(Qg.dma(lambda g, dH=dH: g.dma_start(
                out=s_woF[:, dH, :], in_=d_wo[dH * 128:(dH + 1) * 128, :])))

        wo_ev = {}
        wo_st = {}
        for tt in range(16):
            bs = (tt % 2) * 4
            xb = s_xw[tt % 2]
            tb = s_tm[tt % 2]
            if tt >= 2:
                gate = wo_ev[tt - 2]
                b.y(lambda y, gate=gate: y.wait_ge(m_V.h, gate))
            lsx, lvx = Qy.dma(lambda y, xb=xb, tt=tt: y.dma_start(
                out=xb[:], in_=d_x[tt * 128:(tt + 1) * 128, :]))
            if tt >= 2:
                gate = wo_ev[tt - 2]
                b.t(lambda t, gate=gate: t.wait_ge(m_V.h, gate))
            if tt == 0:
                b.t(lambda t, g=b2_t: t.wait_ge(m_T.h, g))
                b.t(lambda t, g=b2_v: t.wait_ge(m_V.h, g))
            ins = None
            for dH in range(NCH):
                if tt == 0:
                    ls_, lv_ = wo_ld[dH]
                    b.t(lambda t, ls_=ls_, lv_=lv_: t.wait_ge(ls_.h, lv_))
                for bb in range(4):
                    ins = b.t(lambda t, bb=bb, dH=dH, tt=tt, bs=bs: t.matmul(
                        pf[bs + bb][:],
                        s_attnT[:, dH, tt * 128:(tt + 1) * 128],
                        s_woF[:, dH, bb * 512:(bb + 1) * 512],
                        start=(dH == 0), stop=(dH == NCH - 1)))
            m_T.inc(ins)
            won = m_T.n
            b.v(lambda v, lsx=lsx, lvx=lvx: v.wait_ge(lsx.h, lvx))
            if tt >= 2:
                gs_, gv_ = wo_st[tt - 2]
                b.v(lambda v, gs_=gs_, gv_=gv_: v.wait_ge(gs_.h, gv_))
            b.v(lambda v, won=won: v.wait_ge(m_T.h, won))
            if tt == 0:
                b.v(lambda v, g=b2_v: v.wait_ge(m_V.h, g))
            for bb in range(4):
                b.v(lambda v, bb=bb, xb=xb, tb=tb, bs=bs: v.scalar_tensor_tensor(
                    out=tb[:, bb * 512:(bb + 1) * 512],
                    in0=pf[bs + bb][:], scalar=1.0,
                    in1=xb[:, bb * 512:(bb + 1) * 512],
                    op0=ALU.mult, op1=ALU.add))
            b.v(lambda v: v.drain())
            ins = None
            for bb in range(4):
                ins = b.v(lambda v, bb=bb, tb=tb: v.tensor_tensor(
                    out=tb[:, bb * 512:(bb + 1) * 512],
                    in0=tb[:, bb * 512:(bb + 1) * 512],
                    in1=s_b2B[:, bb * 512:(bb + 1) * 512], op=ALU.add))
            m_V.inc(ins)
            wo_ev[tt] = m_V.n
            ev = m_V.n
            b.s(lambda sc, ev=ev: sc.wait_ge(m_V.h, ev))
            wo_st[tt] = Qs.dma(lambda sc, tb=tb, tt=tt: sc.dma_start(
                out=d_tmp2[tt * 128:(tt + 1) * 128, :], in_=tb[:]))

        pf, _ = psum_pool(8)
        for em in (b.y, b.g, b.s, b.v, b.t):
            Qs.wait_all(em)
            Qg.wait_all(em)
            Qy.wait_all(em)
        b.flush()
        scopes["SCW"].close()
        scopes["SBC"].close()
        pfT = pf[4:8]

        s_H = [sb(f"s_H{i}", (128, INNER), F32, "SC") for i in range(2)]
        s_b1B = sb("s_b1B", (128, INNER), BF16, "SC")
        s_x1T = [sb(f"s_x1T{i}", (128, NCH, 128), BF16, "SC") for i in range(2)]
        s_x2T = sb("s_x2T", (128, 64, 128), BF16, "SC")
        s_w1b = [sb(f"s_w1b{i}", (128, 4, 1024), FP8, "SC") for i in range(4)]
        s_w2b = [sb(f"s_w2b{i}", (128, 2, D), FP8, "SC") for i in range(4)]
        s_xc = [sb(f"s_xc{i}", (128, D), F32, "SC") for i in range(2)]
        s_t2 = [sb(f"s_t2{i}", (128, D), F32, "SC") for i in range(2)]
        s_ot = [sb(f"s_ot{i}", (128, D), F32, "SC") for i in range(2)]
        s_fc = [sb(f"s_fc{i}", (128, 24), F32, "SC") for i in range(2)]
        s_qf = sb("s_qf", (128, 112), F32, "SC")
        s_idf = sb("s_idf", (128, 128), F32, "SC")
        idf_ls, idf_lv = Qy.dma(lambda y: y.dma_start(out=s_idf[:], in_=d_identf))

        def rsqrt_wide(tsr, src, dst, w, scr):
            b.v(lambda v: v.reciprocal(out=tsr[:, scr:scr + w],
                                       in_=tsr[:, src:src + w]))
            b.v(lambda v: v.drain())
            m_V.inc(b.v(lambda v: v.tensor_copy(out=tsr[:, scr + w:scr + w + 1],
                                                in_=tsr[:, scr:scr + 1])))
            vv = m_V.n
            b.s(lambda sc, vv=vv: sc.wait_ge(m_V.h, vv))
            m_S.inc(b.s(lambda sc: sc.activation(out=tsr[:, dst:dst + w],
                                                 in_=tsr[:, scr:scr + w],
                                                 func=AF.Sqrt)))
            ss = m_S.n
            b.v(lambda v, ss=ss: v.wait_ge(m_S.h, ss))
            b.v(lambda v: v.tensor_mul(out=tsr[:, scr:scr + w],
                                       in0=tsr[:, dst:dst + w],
                                       in1=tsr[:, dst:dst + w]))
            b.v(lambda v: v.drain())
            b.v(lambda v: v.tensor_mul(out=tsr[:, scr:scr + w],
                                       in0=tsr[:, scr:scr + w],
                                       in1=tsr[:, src:src + w]))
            b.v(lambda v: v.drain())
            b.v(lambda v: v.tensor_scalar(out=tsr[:, scr:scr + w],
                                          in0=tsr[:, scr:scr + w],
                                          scalar1=-0.5, scalar2=1.5,
                                          op0=ALU.mult, op1=ALU.add))
            b.v(lambda v: v.drain())
            b.v(lambda v: v.tensor_mul(out=tsr[:, dst:dst + w],
                                       in0=tsr[:, dst:dst + w],
                                       in1=tsr[:, scr:scr + w]))
            b.v(lambda v: v.drain())

        b.v(lambda v: v.tensor_mul(out=s_qf[:, 48:64], in0=s_lnv[:], in1=s_lnr[:]))
        b.v(lambda v: v.drain())
        b.v(lambda v: v.tensor_mul(out=s_qf[:, 48:64], in0=s_qf[:, 48:64],
                                   in1=s_lnr[:]))
        b.v(lambda v: v.drain())
        b.v(lambda v: v.tensor_scalar(out=s_qf[:, 48:64], in0=s_qf[:, 48:64],
                                      scalar1=1.0, scalar2=EPS_RMS,
                                      op0=ALU.mult, op1=ALU.add))
        b.v(lambda v: v.drain())
        rsqrt_wide(s_qf, 48, 64, 16, 80)
        b.v(lambda v: v.tensor_mul(out=s_qf[:, 80:96], in0=s_qf[:, 48:64],
                                   in1=s_qf[:, 64:80]))
        b.v(lambda v: v.drain())
        b.v(lambda v: v.tensor_scalar_mul(out=s_qf[:, 48:64], in0=s_qf[:, 80:96],
                                          scalar1=EPS_Q * C1))
        b.v(lambda v: v.drain())
        b.v(lambda v: v.tensor_max(out=s_qf[:, 48:64], in0=s_qf[:, 48:64],
                                   in1=s_mx1[:]))
        b.v(lambda v: v.drain())
        b.v(lambda v: v.tensor_mul(out=s_qf[:, 32:48], in0=s_qf[:, 48:64],
                                   in1=s_qf[:, 64:80]))
        b.v(lambda v: v.drain())
        b.v(lambda v: v.tensor_scalar(out=s_qf[:, 32:48], in0=s_qf[:, 32:48],
                                      scalar1=s_pw[:, 2:3],
                                      scalar2=1.0 / (127.0 * C1),
                                      op0=ALU.mult, op1=ALU.mult))
        b.v(lambda v: v.drain())
        b.v(lambda v: v.reciprocal(out=s_qf[:, 80:96], in_=s_qf[:, 48:64]))
        b.v(lambda v: v.drain())
        b.v(lambda v: v.tensor_scalar_mul(out=s_qf[:, 80:96], in0=s_qf[:, 80:96],
                                          scalar1=127.0))
        b.v(lambda v: v.drain())
        b.v(lambda v: v.tensor_mul(out=s_qf[:, 0:16], in0=s_lnr[:],
                                   in1=s_qf[:, 80:96]))
        b.v(lambda v: v.drain())
        b.v(lambda v: v.tensor_mul(out=s_qf[:, 16:32], in0=s_lnm[:],
                                   in1=s_qf[:, 0:16]))
        b.v(lambda v: v.drain())
        m_V.inc(b.v(lambda v: v.tensor_scalar(out=s_qf[:, 16:32],
                                              in0=s_qf[:, 16:32],
                                              scalar1=-1.0,
                                              scalar2=s_mgc[:, 0:1],
                                              op0=ALU.mult, op1=ALU.add)))
        qf_v = m_V.n

        ev_b = 0
        row_t = 0
        for i in range(INNER // 512):
            b.y(lambda y, g=row_t: y.wait_ge(m_T.h, g))
            ls_, lv_ = Qy.dma(lambda y, i=i: y.dma_start(
                out=s_ot[0][0:1, 0:512], in_=d_b1[0:1, i * 512:(i + 1) * 512]))
            pp = pf[i % 2]
            b.t(lambda t, ls_=ls_, lv_=lv_: t.wait_ge(ls_.h, lv_))
            if ev_b:
                b.t(lambda t, ev_b=ev_b: t.wait_ge(m_V.h, ev_b))
            m_T.inc(b.t(lambda t, pp=pp: t.matmul(
                pp[:], s_onesf[0:1, :], s_ot[0][0:1, 0:512],
                start=True, stop=True)))
            row_t = m_T.n
            b.v(lambda v, tn=row_t: v.wait_ge(m_T.h, tn))
            b.v(lambda v: v.drain())
            m_V.inc(b.v(lambda v, pp=pp, i=i: v.tensor_copy(
                out=s_b1B[:, i * 512:(i + 1) * 512], in_=pp[:])))
            ev_b = m_V.n
        b1_v = m_V.n
        b1_t = m_T.n

        def transpose_f32(src_tensor, nch, dst_fn):
            for bt in range(nch // 4):
                gate = m_V.n
                b.t(lambda t, gate=gate: t.wait_ge(m_V.h, gate))
                ins = None
                for kk in range(4):
                    cc = bt * 4 + kk
                    ins = b.t(lambda t, kk=kk, cc=cc,
                              src_tensor=src_tensor: t.transpose(
                        pfT[kk][:, 0:128],
                        src_tensor[:, cc * 128:(cc + 1) * 128], s_idf[:]))
                m_T.inc(ins)
                trn = m_T.n
                b.v(lambda v, trn=trn: v.wait_ge(m_T.h, trn))
                b.v(lambda v: v.drain())
                ins = None
                for kk in range(4):
                    cc = bt * 4 + kk
                    ins = b.v(lambda v, kk=kk, cc=cc: dst_fn(v, cc,
                                                            pfT[kk][:, 0:128]))
                m_V.inc(ins)

        tr1_t, vm_v, x1t_v, q1_s = {}, {}, {}, {}
        f1mm, f1ev = {}, {}
        tr2_t, x2t_v, f2mm, f2_t = {}, {}, {}, {}
        vmag_v, epi_v, ssq_s, st_d, t2_ld = {}, {}, {}, {}, {}

        x_ld = {}

        def x_load(k):
            xb = s_xc[k % 2]
            gate = tr1_t.get(k - 2, 0)
            b.y(lambda y, gate=gate: y.wait_ge(m_T.h, gate))
            x_ld[k] = Qy.dma(lambda y, xb=xb, k=k: y.dma_start(
                out=xb[:], in_=d_x[k * 128:(k + 1) * 128, :]))

        def stage_q1(k):
            xb = s_xc[k % 2]
            if k == 0:
                x_load(0)
            if k + 1 < 16:
                x_load(k + 1)
            lsx, lvx = x_ld[k]
            b.s(lambda sc, lsx=lsx, lvx=lvx: sc.wait_ge(lsx.h, lvx))
            if k == 0:
                b.s(lambda sc, g=qf_v: sc.wait_ge(m_V.h, g))
            b.s(lambda sc: sc.drain())
            m_S.inc(b.s(lambda sc, xb=xb, k=k: sc.activation(
                out=xb[:], in_=xb[:], func=AF.Identity,
                bias=s_qf[:, 16 + k:17 + k], scale=s_qf[:, k:k + 1])))
            b.s(lambda sc: sc.drain())
            m_S.inc(b.s(lambda sc, xb=xb: sc.activation(
                out=xb[:], in_=xb[:], func=AF.Identity,
                bias=s_mgcn[:, 0:1], scale=1.0)))
            q1_s[k] = m_S.n
            vm_v[k] = m_S.n
            tgate = vm_v[k]
            b.t(lambda t, tgate=tgate: t.wait_ge(m_S.h, tgate))
            egate = epi_v.get(k - 3, 0)
            b.t(lambda t, egate=egate: t.wait_ge(m_V.h, egate))
            if k == 0:
                b.t(lambda t, ls_=idf_ls, lv_=idf_lv: t.wait_ge(ls_.h, lv_))
            transpose_f32(s_xc[k % 2], NCH,
                          lambda v, cc, src, k=k: v.tensor_copy(
                              out=s_x1T[k % 2][:, cc, :], in_=src))
            tr1_t[k] = m_T.n
            x1t_v[k] = m_V.n

        def stage_f1(k):
            for ib in range(8):
                for d4 in range(4):
                    k4 = ib * 4 + d4
                    wb1 = s_w1b[k4 % 4]
                    gate = f1mm.get((k, k4 - 4), f1mm.get((k - 1, k4 + 28), 0))
                    if k4 % 2 == 0:
                        b.y(lambda y, gate=gate: y.wait_ge(m_T.h, gate))
                        ls_, lv_ = Qy.dma(lambda y, wb1=wb1, k4=k4: y.dma_start(
                            out=wb1[:], in_=d_w1s[k4 * 512:(k4 + 1) * 512, :]
                            .rearrange("(c p) i -> p c i", p=128)))
                    else:
                        b.s(lambda sc, gate=gate: sc.wait_ge(m_T.h, gate))
                        ls_, lv_ = Qs.dma(lambda sc, wb1=wb1, k4=k4: sc.dma_start(
                            out=wb1[:], in_=d_w1s[k4 * 512:(k4 + 1) * 512, :]
                            .rearrange("(c p) i -> p c i", p=128)))
                    b.t(lambda t, ls_=ls_, lv_=lv_: t.wait_ge(ls_.h, lv_))
                    if d4 == 0:
                        gate = f1ev.get((k, ib - 2), f1ev.get((k - 1, ib + 6), 0))
                        b.t(lambda t, gate=gate: t.wait_ge(m_V.h, gate))
                        if ib == 0:
                            b.t(lambda t, g=x1t_v[k]: t.wait_ge(m_V.h, g))
                            if k == 0:
                                b.t(lambda t, g=b1_v: t.wait_ge(m_V.h, g))
                    ins = None
                    for sub in range(4):
                        dd = d4 * 4 + sub
                        for hh in range(2):
                            ins = b.t(lambda t, ib=ib, dd=dd, sub=sub, hh=hh,
                                      wb1=wb1, k=k: t.matmul(
                                pf[(ib % 2) * 2 + hh][:],
                                s_x1T[k % 2][:, dd, :],
                                wb1[:, sub, hh * 512:(hh + 1) * 512],
                                start=(dd == 0), stop=(dd == NCH - 1)))
                    m_T.inc(ins)
                    f1mm[(k, k4)] = m_T.n
                f1n = m_T.n
                b.v(lambda v, f1n=f1n: v.wait_ge(m_T.h, f1n))
                ins = None
                for hh in range(2):
                    ins = b.v(lambda v, ib=ib, hh=hh, k=k: v.scalar_tensor_tensor(
                        out=s_H[k % 2][:, ib * 1024 + hh * 512:
                                       ib * 1024 + (hh + 1) * 512],
                        in0=pf[(ib % 2) * 2 + hh][:],
                        scalar=s_qf[:, 32 + k:33 + k],
                        in1=s_b1B[:, ib * 1024 + hh * 512:
                                  ib * 1024 + (hh + 1) * 512],
                        op0=ALU.mult, op1=ALU.add))
                m_V.inc(ins)
                f1ev[(k, ib)] = m_V.n

        def rsqrt_col(fc, src, dst):
            b.v(lambda v: v.reciprocal(out=fc[:, 20:21], in_=fc[:, src:src + 1]))
            b.v(lambda v: v.drain())
            m_V.inc(b.v(lambda v: v.tensor_copy(out=fc[:, 21:22],
                                                in_=fc[:, 20:21])))
            vv = m_V.n
            b.s(lambda sc, vv=vv: sc.wait_ge(m_V.h, vv))
            m_S.inc(b.s(lambda sc: sc.activation(out=fc[:, dst:dst + 1],
                                                 in_=fc[:, 20:21],
                                                 func=AF.Sqrt)))
            ss = m_S.n
            b.v(lambda v, ss=ss: v.wait_ge(m_S.h, ss))
            b.v(lambda v: v.tensor_mul(out=fc[:, 20:21], in0=fc[:, dst:dst + 1],
                                       in1=fc[:, dst:dst + 1]))
            b.v(lambda v: v.drain())
            b.v(lambda v: v.tensor_mul(out=fc[:, 20:21], in0=fc[:, 20:21],
                                       in1=fc[:, src:src + 1]))
            b.v(lambda v: v.drain())
            b.v(lambda v: v.tensor_scalar(out=fc[:, 20:21], in0=fc[:, 20:21],
                                          scalar1=-0.5, scalar2=1.5,
                                          op0=ALU.mult, op1=ALU.add))
            b.v(lambda v: v.drain())
            b.v(lambda v: v.tensor_mul(out=fc[:, dst:dst + 1],
                                       in0=fc[:, dst:dst + 1],
                                       in1=fc[:, 20:21]))
            b.v(lambda v: v.drain())

        def stage_mid(k):
            H = s_H[k % 2]
            fc = s_fc[k % 2]
            jk = s_ot[k % 2]
            lastev = f1ev[(k, 7)]
            b.s(lambda sc, g=lastev: sc.wait_ge(m_V.h, g))
            b.s(lambda sc: sc.drain())
            ins = None
            for ib in range(8):
                ins = b.s(lambda sc, ib=ib, H=H: sc.activation(
                    out=H[:, ib * 1024:(ib + 1) * 1024],
                    in_=H[:, ib * 1024:(ib + 1) * 1024], func=AF.Gelu))
            m_S.inc(ins)
            gel = m_S.n
            b.v(lambda v, gel=gel: v.wait_ge(m_S.h, gel))
            m_V.inc(b.v(lambda v, H=H, fc=fc: v.reduce_max(
                out=fc[:, 6:7], in_=H[:], axis=AX.X,
                apply_absolute_value=True)))
            if (k - 2) in st_d:
                gs_, gv_ = st_d[k - 2]
                b.s(lambda sc, gs_=gs_, gv_=gv_: sc.wait_ge(gs_.h, gv_))
            if k == 0:
                b.s(lambda sc, g=b1_t: sc.wait_ge(m_T.h, g))
            ins = None
            for q4 in range(4):
                b.s(lambda sc: sc.drain())
                ins = b.s(lambda sc, q4=q4, H=H, jk=jk, fc=fc: sc.activation(
                    out=jk[:],
                    in_=H[:, q4 * 2048:(q4 + 1) * 2048], func=AF.Square,
                    accum_out=fc[:, 8 + q4:9 + q4]))
            m_S.inc(ins)
            ssq_s[k] = m_S.n
            ssqn = m_S.n
            b.v(lambda v, ssqn=ssqn: v.wait_ge(m_S.h, ssqn))
            b.v(lambda v, fc=fc: v.tensor_add(out=fc[:, 8:9], in0=fc[:, 8:9],
                                              in1=fc[:, 9:10]))
            b.v(lambda v: v.drain())
            b.v(lambda v, fc=fc: v.tensor_add(out=fc[:, 10:11], in0=fc[:, 10:11],
                                              in1=fc[:, 11:12]))
            b.v(lambda v: v.drain())
            b.v(lambda v, fc=fc: v.tensor_add(out=fc[:, 8:9], in0=fc[:, 8:9],
                                              in1=fc[:, 10:11]))
            b.v(lambda v: v.drain())
            b.v(lambda v, fc=fc: v.tensor_scalar(out=fc[:, 0:1], in0=fc[:, 8:9],
                                                 scalar1=1.0 / INNER,
                                                 scalar2=EPS_RMS,
                                                 op0=ALU.mult, op1=ALU.add))
            b.v(lambda v: v.drain())
            rsqrt_col(fc, 0, 1)
            b.v(lambda v, fc=fc: v.tensor_mul(out=fc[:, 2:3], in0=fc[:, 0:1],
                                              in1=fc[:, 1:2]))
            b.v(lambda v: v.drain())
            b.v(lambda v, fc=fc: v.tensor_scalar_mul(out=fc[:, 3:4],
                                                     in0=fc[:, 2:3],
                                                     scalar1=EPS_Q * C2))
            b.v(lambda v: v.drain())
            b.v(lambda v, fc=fc: v.tensor_max(out=fc[:, 3:4], in0=fc[:, 3:4],
                                              in1=fc[:, 6:7]))
            b.v(lambda v: v.drain())
            b.v(lambda v, fc=fc: v.reciprocal(out=fc[:, 4:5], in_=fc[:, 3:4]))
            b.v(lambda v: v.drain())
            b.v(lambda v, fc=fc: v.tensor_scalar_mul(out=fc[:, 4:5],
                                                     in0=fc[:, 4:5],
                                                     scalar1=127.0))
            b.v(lambda v: v.drain())
            b.v(lambda v, fc=fc: v.tensor_mul(out=fc[:, 5:6], in0=fc[:, 3:4],
                                              in1=fc[:, 1:2]))
            b.v(lambda v: v.drain())
            b.v(lambda v, fc=fc: v.tensor_mul(out=fc[:, 5:6], in0=fc[:, 5:6],
                                              in1=s_pw[:, 3:4]))
            b.v(lambda v: v.drain())
            m_V.inc(b.v(lambda v, fc=fc: v.tensor_scalar_mul(
                out=fc[:, 5:6], in0=fc[:, 5:6],
                scalar1=1.0 / (127.0 * C2))))
            vq2 = m_V.n
            b.s(lambda sc, vq2=vq2: sc.wait_ge(m_V.h, vq2))
            b.s(lambda sc: sc.drain())
            m_S.inc(b.s(lambda sc, H=H, fc=fc: sc.activation(
                out=H[:], in_=H[:], func=AF.Identity,
                bias=s_mgc[:, 0:1], scale=fc[:, 4:5])))
            b.s(lambda sc: sc.drain())
            m_S.inc(b.s(lambda sc, H=H: sc.activation(
                out=H[:], in_=H[:], func=AF.Identity,
                bias=s_mgcn[:, 0:1], scale=1.0)))
            vmag_v[k] = m_S.n

        def stage_f2(k):
            gate = epi_v.get(k - 2, 0)
            b.y(lambda y, gate=gate: y.wait_ge(m_V.h, gate))
            t2_ld[k] = Qy.dma(lambda y, k=k: y.dma_start(
                out=s_t2[k % 2][:], in_=d_tmp2[k * 128:(k + 1) * 128, :]))
            g1 = vmag_v[k]
            b.t(lambda t, g1=g1: t.wait_ge(m_S.h, g1))
            transpose_f32(s_H[k % 2], 64,
                          lambda v, cc, src: v.tensor_copy(out=s_x2T[:, cc, :],
                                                           in_=src))
            tr2_t[k] = m_T.n
            x2t_v[k] = m_V.n
            b.t(lambda t, g=x2t_v[k]: t.wait_ge(m_V.h, g))
            for ic2 in range(32):
                wb2 = s_w2b[ic2 % 4]
                gate = f2mm.get((k, ic2 - 4), f2mm.get((k - 1, ic2 + 28), 0))
                if ic2 < 4:
                    b.s(lambda sc, gate=gate: sc.wait_ge(m_T.h, gate))
                    ls_, lv_ = Qs.dma(lambda sc, wb2=wb2, ic2=ic2: sc.dma_start(
                        out=wb2[:], in_=d_w2s[ic2 * 256:(ic2 + 1) * 256, :]
                        .rearrange("(c p) o -> p c o", p=128)))
                else:
                    b.g(lambda g, gate=gate: g.wait_ge(m_T.h, gate))
                    ls_, lv_ = Qg.dma(lambda g, wb2=wb2, ic2=ic2: g.dma_start(
                        out=wb2[:], in_=d_w2s[ic2 * 256:(ic2 + 1) * 256, :]
                        .rearrange("(c p) o -> p c o", p=128)))
                b.t(lambda t, ls_=ls_, lv_=lv_: t.wait_ge(ls_.h, lv_))
                ins = None
                for sub in range(2):
                    ic = ic2 * 2 + sub
                    for bb in range(4):
                        ins = b.t(lambda t, bb=bb, ic=ic, sub=sub,
                                  wb2=wb2: t.matmul(
                            pf[4 + bb][:], s_x2T[:, ic, :],
                            wb2[:, sub, bb * 512:(bb + 1) * 512],
                            start=(ic == 0), stop=(ic == 63)))
                m_T.inc(ins)
                f2mm[(k, ic2)] = m_T.n
            f2_t[k] = m_T.n
            lst, lvt = t2_ld[k]
            b.v(lambda v, lst=lst, lvt=lvt: v.wait_ge(lst.h, lvt))
            if (k - 2) in st_d:
                gs_, gv_ = st_d[k - 2]
                b.v(lambda v, gs_=gs_, gv_=gv_: v.wait_ge(gs_.h, gv_))
            fn = f2_t[k]
            b.v(lambda v, fn=fn: v.wait_ge(m_T.h, fn))
            ins = None
            for bb in range(4):
                ins = b.v(lambda v, bb=bb, k=k: v.scalar_tensor_tensor(
                    out=s_ot[k % 2][:, bb * 512:(bb + 1) * 512],
                    in0=pf[4 + bb][:], scalar=s_fc[k % 2][:, 5:6],
                    in1=s_t2[k % 2][:, bb * 512:(bb + 1) * 512],
                    op0=ALU.mult, op1=ALU.add))
            m_V.inc(ins)
            epi_v[k] = m_V.n

        def emit_store(k):
            ev = epi_v[k]
            b.s(lambda sc, ev=ev: sc.wait_ge(m_V.h, ev))
            st_d[k] = Qst.dma(lambda sc, k=k: sc.dma_start(
                out=d_out[k * 128:(k + 1) * 128, :], in_=s_ot[k % 2][:]))

        stage_q1(0)
        stage_q1(1)
        stage_f1(0)
        for k in range(16):
            stage_mid(k)
            if k + 1 < 16:
                stage_f1(k + 1)
            if k + 2 < 16:
                stage_q1(k + 2)
            if k - 1 >= 0:
                emit_store(k - 1)
            stage_f2(k)
        emit_store(15)

    if "A" in stages:
        pfA, ptrbA = psum_pool(6, 1)
        phase_A_half(0, ptrbA[0])
        proj_half(0, pfA)
        phase_A_half(1, ptrbA[0])
        proj_half(1, pfA)

    if "B" in stages:
        phase_B()

    if "C" in stages:
        phase_C()

    if "attn" in dbg:
        o_a = dbg_out("attnT", (128, NCH * T), BF16)
        vn2, sn2 = m_V.n, m_S.n
        b.y(lambda y, vn2=vn2: y.wait_ge(m_V.h, vn2))
        b.y(lambda y, sn2=sn2: y.wait_ge(m_S.h, sn2))
        Qst.dma(lambda y: y.dma_start(out=o_a,
                                      in_=s_attnT[:].rearrange("p c t -> p (c t)")))

    if "qkv" in dbg:
        o_q = dbg_out("qF", (128, NCH * T), BF16)
        o_k = dbg_out("kF", (NKV * 128, T), BF16)
        o_v = dbg_out("vF", (128, NCH * NKV * (HD + 1)), BF16)
        o_ln = dbg_out("ln", (128, 4 * NCH))
        vn, sn = m_V.n, m_S.n
        b.y(lambda y, vn=vn: y.wait_ge(m_V.h, vn))
        b.y(lambda y, sn=sn: y.wait_ge(m_S.h, sn))
        for ks_, kv_ in kdup_waits:
            b.y(lambda y, ks_=ks_, kv_=kv_: y.wait_ge(ks_.h, kv_))
        Qst.dma(lambda y: y.dma_start(out=o_q, in_=s_qF[:].rearrange("p c t -> p (c t)")))
        for g in range(NKV):
            Qst.dma(lambda y, g=g: y.dma_start(out=o_k[g * 128:(g + 1) * 128, :],
                                               in_=s_kF[g][:]))
        Qst.dma(lambda y: y.dma_start(out=o_v, in_=s_vF[:].rearrange("p c t -> p (c t)")))
        Qst.dma(lambda y: y.dma_start(out=o_ln[:, 0:NCH], in_=s_lnm[:]))
        Qst.dma(lambda y: y.dma_start(out=o_ln[:, NCH:2 * NCH], in_=s_lnr[:]))
        Qst.dma(lambda y: y.dma_start(out=o_ln[:, 2 * NCH:3 * NCH], in_=s_lnv[:]))
        Qst.dma(lambda y: y.dma_start(out=o_ln[:, 3 * NCH:4 * NCH], in_=s_mx1[:]))

    if "C" not in stages:
        b.v(lambda v: v.drain())
        tz = m_T.n
        vz = m_V.n
        b.g(lambda g, tz=tz: g.wait_ge(m_T.h, tz))
        b.g(lambda g, vz=vz: g.wait_ge(m_V.h, vz))
        for tt in range(16):
            for q4 in range(4):
                Qg.dma(lambda g, tt=tt, q4=q4: g.dma_start(
                    out=d_out[tt * 128:(tt + 1) * 128, q4 * 512:(q4 + 1) * 512],
                    in_=s_attnT[:, 0, 0:512]))

    for Q in (Qy, Qg, Qs, Qst):
        Q.wait_all(b.y)

    b.flush()
    if pctx[0] is not None:
        pctx[0].close()
        pctx[0] = None
    for k in ("SC", "SB", "SA", "SQKV", "SBC"):
        scopes[k].close()
    ctx.close()
    return nc, d_dbg


def host_inputs(x_b, wq, wk, wv, wo, qnw, knw, w1, b1, w2, b2):
    asc = np.ascontiguousarray
    wqT = asc(wq.T)
    wkvT = asc(np.concatenate([wk, wv], axis=0).T)
    d = dict(
        x_tok=asc(x_b),
        wqT_blk=asc(wqT.reshape(D, 16, 128).transpose(1, 0, 2)).reshape(16 * D, 128),
        wkvT_blk=asc(wkvT.reshape(D, 8, 128).transpose(1, 0, 2)).reshape(8 * D, 128),
        woT=asc(wo.T),
        w1T=asc(w1.T),
        w2T=asc(w2.T),
        qnw_col=asc(qnw.reshape(NCH, 128).T),
        knw_col=asc(knw.reshape(4, 128).T),
        b1_row=asc(b1.reshape(1, INNER)),
        b2_row=asc(b2.reshape(1, D)),
    )
    d.update(_host_consts())
    return d


def kernel(L0_attn_x, L0_attn_wq, L0_attn_wk, L0_attn_wv, L0_attn_wo,
           L0_attn_q_norm_w, L0_attn_k_norm_w, L0_ffn_w1, L0_ffn_b1,
           L0_ffn_w2, L0_ffn_b2):
    nc, _ = build_program()
    in_maps = [host_inputs(L0_attn_x[c], L0_attn_wq, L0_attn_wk, L0_attn_wv,
                           L0_attn_wo, L0_attn_q_norm_w, L0_attn_k_norm_w,
                           L0_ffn_w1, L0_ffn_b1, L0_ffn_w2, L0_ffn_b2)
               for c in range(8)]
    res = run_bass_kernel_spmd(nc, in_maps, core_ids=list(range(8)))
    return np.stack([r["out"] for r in res.results], axis=0)


# version 17 (final)
# speedup vs baseline: 1.0088x; 1.0088x over previous
import numpy as np
import ml_dtypes
import concourse.bass as bass
import concourse.mybir as mybir
from concourse.bass_utils import run_bass_kernel_spmd
from contextlib import ExitStack

F32 = mybir.dt.float32
BF16 = mybir.dt.bfloat16
FP8 = mybir.dt.float8e4
AF = mybir.ActivationFunctionType
ALU = mybir.AluOpType
AX = mybir.AxisListType
BF = ml_dtypes.bfloat16

MAGIC = float(np.float32(1.5 * 2 ** 23))
T = 2048
D = 2048
NH = 32
NKV = 8
HD = 64
WIN = 1024
INNER = 8192
NCH = D // 128
EPS_LN = 1e-5
EPS_QK = 1e-6
EPS_RMS = 1e-6
EPS_Q = 1e-5
WMAX = WIN + 128


def _slopes():
    r = 2.0 ** (-8.0 / NH)
    return np.array([r ** i for i in range(NH)], dtype=np.float64)


def _alibi_strips():
    sl32 = np.float32([np.float64(2.0 ** (-8.0 / NH)) ** i for i in range(NH)])
    p = np.arange(128, dtype=np.int64)[:, None]
    u = np.arange(WMAX, dtype=np.int64)[None, :]
    d = p - u
    strips = np.empty((NH, 128, WMAX), np.float32)
    for h in range(NH):
        val = sl32[h] * d.astype(np.float32)
        strips[h] = np.where((d <= 0) & (d >= -WIN), val, np.float32(-1e30))
    return strips.reshape(NH * 128, WMAX)


def _host_consts():
    return dict(
        c_ident=np.eye(128, dtype=BF),
        c_ones=np.ones((128, 128), dtype=BF),
        c_zeros=np.zeros((1, 512), dtype=ml_dtypes.float8_e4m3),
        alibi_strips=_alibi_strips(),
        c_onesf=np.ones((65, 128), dtype=np.float32),
        c_identf=np.eye(128, dtype=np.float32),
        c_onesc=np.ones((128, 1), dtype=np.float32),
        c_onescb=np.ones((128, 1), dtype=BF),
    )


class Sem:
    def __init__(self, handle):
        self.h = handle
        self.n = 0

    def inc(self, instr, k=1):
        self.n += k
        instr.then_inc(self.h, k)
        return self.n


class DmaQ:
    def __init__(self, emitter, mk_sems, n=4):
        self.sems = mk_sems(n)
        self.emit = emitter
        self.i = 0

    def dma(self, fn):
        s = self.sems[self.i % len(self.sems)]
        self.i += 1
        if s.n > 0:
            self.emit(lambda e, s=s, v=s.n: e.wait_ge(s.h, v))
        ins = self.emit(fn)
        s.inc(ins, 16)
        return s, s.n

    def wait_all(self, emitter):
        for s in self.sems:
            if s.n > 0:
                emitter(lambda e, s=s, v=s.n: e.wait_ge(s.h, v))


class InsProxy:
    __slots__ = ("incs",)

    def __init__(self):
        self.incs = []

    def then_inc(self, sem, k):
        self.incs.append((sem, k))
        return self


class Bld:
    ENGS = ("tensor", "vector", "scalar", "gpsimd", "sync")

    def __init__(self, block):
        self.block = block
        self.pend = {e: [] for e in self.ENGS}

    def _emit(self, eng, fn):
        p = InsProxy()
        self.pend[eng].append((fn, p))
        return p

    def flush(self):
        for eng in self.ENGS:
            lst = self.pend[eng]
            if not lst:
                continue
            self.pend[eng] = []

            def run(e, lst=lst):
                for fn, p in lst:
                    ins = fn(e)
                    if p.incs:
                        assert ins is not None
                        for sem, k in p.incs:
                            ins.then_inc(sem, k)
                return None
            getattr(self.block, eng)(run)

    def t(self, fn):
        return self._emit("tensor", fn)

    def v(self, fn):
        return self._emit("vector", fn)

    def s(self, fn):
        return self._emit("scalar", fn)

    def g(self, fn):
        return self._emit("gpsimd", fn)

    def y(self, fn):
        return self._emit("sync", fn)


def build_program(stages=("A", "B", "C"), debug=()):
    nc = bass.Bass("TRN2", target_bir_lowering=False)
    dbg = set(debug)
    stages = set(stages)

    def dram_in(name, shape, dt=F32):
        return nc.dram_tensor(name, list(shape), dt, kind="ExternalInput").ap()

    d_x = dram_in("x_tok", (T, D))
    d_wq = dram_in("wqT_blk", (16 * D, 128))
    d_wkv = dram_in("wkvT_blk", (8 * D, 128))
    d_wo = dram_in("woT", (D, D))
    d_w1T = dram_in("w1T", (D, INNER))
    d_w2T = dram_in("w2T", (INNER, D))
    d_qnw = dram_in("qnw_col", (128, NCH))
    d_knw = dram_in("knw_col", (128, 4))
    d_b1 = dram_in("b1_row", (1, INNER))
    d_b2 = dram_in("b2_row", (1, D))
    d_ident = dram_in("c_ident", (128, 128), BF16)
    d_ones = dram_in("c_ones", (128, 128), BF16)
    d_zeros = dram_in("c_zeros", (1, 512), FP8)
    d_astrip = dram_in("alibi_strips", (NH * 128, WMAX))
    d_onesf = dram_in("c_onesf", (65, 128))
    d_onescb = dram_in("c_onescb", (128, 1), BF16)
    d_identf = dram_in("c_identf", (128, 128))
    d_onesc = dram_in("c_onesc", (128, 1))

    d_out = nc.dram_tensor("out", [T, D], F32, kind="ExternalOutput").ap()
    d_dbg = {}

    def dbg_out(name, shape, dt=F32):
        d_dbg[name] = nc.dram_tensor("dbg_" + name, list(shape), dt,
                                     kind="ExternalOutput").ap()
        return d_dbg[name]

    d_w1s = nc.dram_tensor("scr_w1s", [8 * NCH * 128, 1024], FP8, kind="Internal").ap()
    d_w2s = nc.dram_tensor("scr_w2s", [64 * 128, D], FP8, kind="Internal").ap()
    d_tmp2 = nc.dram_tensor("scr_tmp2", [T, D], F32, kind="Internal").ap()

    ctx = ExitStack()
    E = ctx.enter_context
    scopes = {k: ExitStack() for k in ("SA", "SQKV", "SB", "SBC", "SCW", "SC")}

    def sb(name, shape, dt=BF16, scope=None, side="left"):
        cm = nc.sbuf_tensor(name, list(shape), dt, side=side)
        if scope is None:
            return E(cm)
        return scopes[scope].enter_context(cm)

    s_id = None
    s_ones = None
    s_onescb = sb("s_onescb", (128, 1))
    s_zeros = sb("s_zeros", (1, 512), FP8)
    s_onesf = sb("s_onesf", (65, 128), F32)
    s_mgc = sb("s_mgc", (128, 1), F32)
    s_mgcn = sb("s_mgcn", (128, 1), F32)


    s_lnm = sb("s_lnm", (128, NCH), F32)
    s_lnr = sb("s_lnr", (128, NCH), F32)
    s_lnv = sb("s_lnv", (128, NCH), F32)
    s_mx1 = sb("s_mx1", (128, NCH), F32)

    s_st = sb("s_st", (128, 32), F32)
    s_pw = sb("s_pw", (128, 8), F32)
    s_onesc = sb("s_onesc", (128, 1), F32)

    s_idfb = sb("s_idfb", (128, 128), F32, "SQKV")
    s_qF = sb("s_qF", (128, NCH, T), BF16, "SQKV")
    s_kF = [sb(f"s_kF{g}", (128, T), BF16, "SQKV") for g in range(NKV)]
    s_vF = sb("s_vF", (128, NCH, NKV * (HD + 1)), BF16, "SQKV")
    s_xnT = sb("s_xnT", (128, NCH, 1024), BF16, "SA")
    s_xt = [sb(f"s_xt{i}", (128, D), F32, "SA") for i in range(2)]
    s_xn = sb("s_xn", (128, D), F32, "SA")
    s_xnb = sb("s_xnb", (128, D), BF16, "SA")
    s_wblk = [sb(f"s_wblk{i}", (128, NCH, 128), BF16, "SA") for i in range(2)]
    s_id = sb("s_id", (128, 128), BF16, "SA")
    s_ones = sb("s_ones", (128, 128), BF16, "SA")
    s_qnw8 = sb("s_qnw8", (128, NCH), F32, "SA")
    s_knw = sb("s_knw", (128, 4), F32, "SA")
    s_bn = sb("s_bn", (128, 24), F32, "SA")
    s_rsq = [sb(f"s_rsq{i}", (128, 512), F32, "SA") for i in range(2)]
    s_ra = sb("s_ra", (128, 512), F32, "SA")
    s_rb = sb("s_rb", (128, 512), F32, "SA")
    s_sq = sb("s_sq", (128, 512), BF16, "SA")

    sems = {}

    def sem(name):
        sems[name] = Sem(E(nc.semaphore(name)))
        return sems[name]

    m_T = sem("m_T")
    m_V = sem("m_V")
    m_S = sem("m_S")
    m_pv = sem("m_pv")
    m_G = sem("m_G")

    blk = E(nc.Block())
    b = Bld(blk)

    def mk_sems(prefix):
        return lambda n: [Sem(E(nc.semaphore(f"{prefix}{i}"))) for i in range(n)]
    Qy = DmaQ(b.y, mk_sems("qy"), 6)
    Qg = DmaQ(b.g, mk_sems("qg"), 6)
    Qs = DmaQ(b.s, mk_sems("qs"), 4)
    Qst = DmaQ(b.s, mk_sems("qst"), 4)

    pctx = [None]
    pool_id = [0]

    def psum_pool(n_f32, n_bf16=0):
        b.flush()
        if pctx[0] is not None:
            pctx[0].close()
            tn, vn, sn = m_T.n, m_V.n, m_S.n
            for emit in (b.t, b.v, b.s, b.g, b.y):
                emit(lambda e, tn=tn: e.wait_ge(m_T.h, tn))
                emit(lambda e, vn=vn: e.wait_ge(m_V.h, vn))
                emit(lambda e, sn=sn: e.wait_ge(m_S.h, sn))
        pctx[0] = ExitStack()
        pid = pool_id[0]
        pool_id[0] += 1
        if isinstance(n_f32, int):
            shapes = [512] * n_f32
        else:
            shapes = n_f32
        pf = [pctx[0].enter_context(nc.psum_tensor(f"pf{pid}_{i}", [128, w], F32))
              for i, w in enumerate(shapes)]
        pbf = [pctx[0].enter_context(nc.psum_tensor(f"pb{pid}_{i}", [128, 1024], BF16))
               for i in range(n_bf16)]
        return pf, pbf

    def flush_banks(pf):
        ins = None
        for pp in pf:
            ins = b.t(lambda t, pp=pp: t.matmul(
                pp[:], s_zeros[0:1, 0:128], s_zeros[0:1, 0:512], start=True, stop=True))
        return ins

    for dst, src in ((s_id, d_ident), (s_ones, d_ones), (s_zeros, d_zeros),
                     (s_knw, d_knw), (s_qnw8, d_qnw), (s_onesf, d_onesf),
                     (s_onesc, d_onesc), (s_onescb, d_onescb),
                     (s_idfb, d_identf)):
        Qy.dma(lambda y, dst=dst, src=src: y.dma_start(out=dst[:], in_=src))
    for em in (b.v, b.t, b.s, b.g):
        Qy.wait_all(em)
    b.v(lambda v: v.tensor_scalar_mul(out=s_qnw8[:], in0=s_qnw8[:], scalar1=0.125))
    b.v(lambda v: v.drain())
    M52 = float(1.5 * 2 ** 52)
    b.v(lambda v: v.memset(s_mgc[:], 1.0))
    b.v(lambda v: v.drain())
    b.v(lambda v: v.tensor_scalar(out=s_mgc[:], in0=s_mgc[:], scalar1=float(2.0 ** -30),
                                  scalar2=None, op0=ALU.add))
    b.v(lambda v: v.drain())
    b.v(lambda v: v.tensor_scalar(out=s_mgc[:], in0=s_mgc[:], scalar1=1.0,
                                  scalar2=None, op0=ALU.subtract))
    b.v(lambda v: v.drain())
    b.v(lambda v: v.tensor_scalar(out=s_mgc[:], in0=s_mgc[:], scalar1=float(2.0 ** 30),
                                  scalar2=None, op0=ALU.mult))
    b.v(lambda v: v.drain())
    b.v(lambda v: v.tensor_scalar(out=s_mgc[:], in0=s_mgc[:], scalar1=float(M52 - MAGIC),
                                  scalar2=MAGIC, op0=ALU.mult, op1=ALU.add))
    b.v(lambda v: v.drain())
    b.v(lambda v: v.tensor_scalar_mul(out=s_mgcn[:], in0=s_mgc[:], scalar1=-1.0))
    b.v(lambda v: v.drain())
    for gg_ in range(NKV):
        b.v(lambda v, gg_=gg_: v.memset(s_vF[:, :, gg_ * 65 + 64:gg_ * 65 + 65], 1.0))
        b.v(lambda v: v.drain())
    m_V.inc(b.v(lambda v: v.memset(s_pw[:], 0.0)))

    prep_state = {}
    prep_rd = {}
    prep_ld = {}
    s_prep = [None, None]
    s_prep8 = [None, None]

    def prep_load(k):
        pi = s_prep[k % 2]
        pi8 = s_prep8[k % 2]
        if k < 64:
            dd, iq = k // 4, k % 4
            src = d_w1T[dd * 128:(dd + 1) * 128, iq * 2048:(iq + 1) * 2048]
            dsts = [(d_w1s[((2 * iq + j) * NCH + dd) * 128:((2 * iq + j) * NCH + dd) * 128 + 128, :],
                     pi8[:, j * 1024:(j + 1) * 1024]) for j in range(2)]
            acc = s_pw[:, 0:1]
        else:
            ic = k - 64
            src = d_w2T[ic * 128:(ic + 1) * 128, :]
            dsts = [(d_w2s[ic * 128:(ic + 1) * 128, :], pi8[:])]
            acc = s_pw[:, 1:2]
        if k >= 1:
            sg_p, va_p = prep_rd[k - 1]
            b.g(lambda g, sg_p=sg_p: g.wait_ge(m_S.h, sg_p))
            if va_p:
                b.g(lambda g, va_p=va_p: g.wait_ge(m_V.h, va_p))
        ls, lv = Qg.dma(lambda g, pi=pi, src=src: g.dma_start(out=pi[:], in_=src))
        prep_ld[k] = (ls, lv, dsts, acc)

    def prep_fin(k):
        pi = s_prep[k % 2]
        pi8 = s_prep8[k % 2]
        ls, lv, dsts, acc = prep_ld[k]
        b.v(lambda v, ls=ls, lv=lv: v.wait_ge(ls.h, lv))
        b.v(lambda v, pi=pi: v.tensor_reduce(out=s_st[:, 20:21], in_=pi[:], axis=AX.X,
                                             op=ALU.add, apply_absolute_value=True))
        b.v(lambda v: v.drain())
        m_V.inc(b.v(lambda v, acc=acc: v.tensor_add(out=acc, in0=acc,
                                                    in1=s_st[:, 20:21])))
        va_n = m_V.n
        b.s(lambda s, ls=ls, lv=lv: s.wait_ge(ls.h, lv))
        if k >= 1:
            for gs, gv in prep_state[k - 1]:
                b.s(lambda s, gs=gs, gv=gv: s.wait_ge(gs.h, gv))
        m_S.inc(b.s(lambda s, pi=pi, pi8=pi8: s.activation(out=pi8[:], in_=pi[:],
                                                           func=AF.Sign)))
        sg_n = m_S.n
        prep_rd[k] = (sg_n, va_n)
        b.s(lambda s, sg_n=sg_n: s.wait_ge(m_S.h, sg_n))
        prep_state[k] = [Qs.dma(lambda s, d=ddst, sr=ssrc: s.dma_start(out=d, in_=sr))
                         for ddst, ssrc in dsts]

    def prep_finalize():
        pf, _ = psum_pool(1)
        vn = m_V.n
        b.t(lambda t, vn=vn: t.wait_ge(m_V.h, vn))
        m_T.inc(b.t(lambda t: t.matmul(pf[0][0:1, 0:2], s_onesc[:, 0:1],
                                       s_pw[:, 0:2], start=True, stop=True)))
        tn = m_T.n
        b.v(lambda v, tn=tn: v.wait_ge(m_T.h, tn))
        b.v(lambda v: v.drain())
        m_V.inc(b.v(lambda v: v.tensor_copy(out=s_st[0:1, 22:24],
                                            in_=pf[0][0:1, 0:2])))
        vn2 = m_V.n
        b.t(lambda t, vn2=vn2: t.wait_ge(m_V.h, vn2))
        m_T.inc(b.t(lambda t: t.matmul(pf[0][:, 0:2], s_onesf[0:1, :],
                                       s_st[0:1, 22:24], start=True, stop=True)))
        tn = m_T.n
        b.v(lambda v, tn=tn: v.wait_ge(m_T.h, tn))
        b.v(lambda v: v.tensor_scalar_mul(out=s_pw[:, 2:3], in0=pf[0][:, 0:1],
                                          scalar1=1.0 / (INNER * D)))
        b.v(lambda v: v.drain())
        m_V.inc(b.v(lambda v: v.tensor_scalar_mul(out=s_pw[:, 3:4], in0=pf[0][:, 1:2],
                                                  scalar1=1.0 / (INNER * D))))

    a_done = {}

    def phase_A_half(half, p_trb):
        for ti in range(8):
            tt = half * 8 + ti
            buf = s_xt[ti % 2]
            if tt >= 2:
                gate = a_done[f"xn{tt - 2}"]
                b.y(lambda y, gate=gate: y.wait_ge(m_V.h, gate))
            lsx, lvx = Qy.dma(lambda y, buf=buf, tt=tt: y.dma_start(
                out=buf[:], in_=d_x[tt * 128:(tt + 1) * 128, :]))
            b.v(lambda v, lsx=lsx, lvx=lvx: v.wait_ge(lsx.h, lvx))
            if tt >= 1:
                gate = a_done[f"ev{tt - 1}_1"]
                b.v(lambda v, gate=gate: v.wait_ge(m_V.h, gate))
            for q4 in range(4):
                b.v(lambda v, buf=buf, q4=q4: v.bn_stats(
                    out=s_bn[:, q4 * 6:(q4 + 1) * 6],
                    in_=buf[:, q4 * 512:(q4 + 1) * 512]))
            b.v(lambda v: v.drain())
            b.v(lambda v: v.bn_aggr(out=s_st[:, 0:2], in_=s_bn[:]))
            b.v(lambda v: v.drain())
            b.v(lambda v, tt=tt: v.tensor_copy(out=s_lnm[:, tt:tt + 1], in_=s_st[:, 0:1]))
            b.v(lambda v, tt=tt: v.tensor_copy(out=s_lnv[:, tt:tt + 1], in_=s_st[:, 1:2]))
            b.v(lambda v: v.tensor_scalar(out=s_st[:, 2:3], in0=s_st[:, 1:2],
                                          scalar1=EPS_LN, scalar2=None, op0=ALU.add))
            b.v(lambda v: v.drain())
            m_V.inc(b.v(lambda v: v.reciprocal(out=s_st[:, 4:5], in_=s_st[:, 2:3])))
            v1 = m_V.n
            b.s(lambda s, v1=v1: s.wait_ge(m_V.h, v1))
            m_S.inc(b.s(lambda s: s.activation(out=s_st[:, 5:6], in_=s_st[:, 4:5],
                                               func=AF.Sqrt)))
            s1 = m_S.n
            b.v(lambda v, s1=s1: v.wait_ge(m_S.h, s1))
            b.v(lambda v: v.tensor_mul(out=s_st[:, 7:8], in0=s_st[:, 5:6],
                                       in1=s_st[:, 5:6]))
            b.v(lambda v: v.drain())
            b.v(lambda v: v.tensor_mul(out=s_st[:, 8:9], in0=s_st[:, 7:8],
                                       in1=s_st[:, 2:3]))
            b.v(lambda v: v.drain())
            b.v(lambda v: v.tensor_scalar(out=s_st[:, 9:10], in0=s_st[:, 8:9],
                                          scalar1=-0.5, scalar2=1.5,
                                          op0=ALU.mult, op1=ALU.add))
            b.v(lambda v: v.drain())
            b.v(lambda v, tt=tt: v.tensor_mul(out=s_lnr[:, tt:tt + 1],
                                              in0=s_st[:, 5:6], in1=s_st[:, 9:10]))
            b.v(lambda v: v.drain())
            b.v(lambda v, buf=buf, tt=tt: v.tensor_scalar(
                out=s_xn[:], in0=buf[:], scalar1=s_lnm[:, tt:tt + 1],
                scalar2=s_lnr[:, tt:tt + 1], op0=ALU.subtract, op1=ALU.mult))
            b.v(lambda v: v.drain())
            b.v(lambda v, tt=tt: v.reduce_max(out=s_mx1[:, tt:tt + 1], in_=s_xn[:],
                                              axis=AX.X, apply_absolute_value=True))
            m_V.inc(b.v(lambda v: v.tensor_copy(out=s_xnb[:], in_=s_xn[:])))
            a_done[f"xn{tt}"] = m_V.n
            v2 = m_V.n
            b.t(lambda t, v2=v2: t.wait_ge(m_V.h, v2))
            for bb in range(2):
                gate = a_done.get(f"ev{tt}_{bb - 1}", a_done.get(f"ev{tt - 1}_1", 0))
                b.t(lambda t, gate=gate: t.wait_ge(m_V.h, gate))
                ins = None
                for kk in range(8):
                    c = bb * 8 + kk
                    ins = b.t(lambda t, kk=kk, c=c, p_trb=p_trb: t.transpose(
                        p_trb[:, kk * 128:(kk + 1) * 128],
                        s_xnb[:, c * 128:(c + 1) * 128], s_id[:]))
                m_T.inc(ins)
                tb = m_T.n
                b.v(lambda v, tb=tb: v.wait_ge(m_T.h, tb))
                b.v(lambda v: v.drain())
                m_V.inc(b.v(lambda v, bb=bb, ti=ti, p_trb=p_trb: v.tensor_copy(
                    out=s_xnT[:, bb * 8:(bb + 1) * 8, ti * 128:(ti + 1) * 128],
                    in_=p_trb[:].rearrange("p (c t) -> p c t", c=8))))
                a_done[f"ev{tt}_{bb}"] = m_V.n

    def rsq512(psq, nfeat, eps, which):
        dst = s_rsq[which]
        b.v(lambda v, psq=psq, dst=dst, nfeat=nfeat, eps=eps: v.tensor_scalar(
            out=dst[:], in0=psq[:], scalar1=1.0 / nfeat, scalar2=eps,
            op0=ALU.mult, op1=ALU.add))
        b.v(lambda v: v.drain())
        m_V.inc(b.v(lambda v, dst=dst: v.reciprocal(out=s_ra[:], in_=dst[:])))
        vv = m_V.n
        b.s(lambda s, vv=vv: s.wait_ge(m_V.h, vv))
        m_S.inc(b.s(lambda s: s.activation(out=s_rb[:], in_=s_ra[:], func=AF.Sqrt)))
        ss = m_S.n
        b.v(lambda v, ss=ss: v.wait_ge(m_S.h, ss))
        b.v(lambda v: v.tensor_mul(out=s_ra[:], in0=s_rb[:], in1=s_rb[:]))
        b.v(lambda v: v.drain())
        b.v(lambda v, dst=dst: v.tensor_mul(out=s_ra[:], in0=s_ra[:], in1=dst[:]))
        b.v(lambda v: v.drain())
        b.v(lambda v: v.tensor_scalar(out=s_ra[:], in0=s_ra[:], scalar1=-0.5,
                                      scalar2=1.5, op0=ALU.mult, op1=ALU.add))
        b.v(lambda v: v.drain())
        m_V.inc(b.v(lambda v, dst=dst: v.tensor_mul(out=dst[:], in0=s_rb[:],
                                                    in1=s_ra[:])))
        b.v(lambda v: v.drain())
        return dst

    def proj_half(half, pf):
        p_pr = pf[0:2]
        p_sq_q = pf[2:4]
        p_sq_k = pf[4:6]
        ev = {}
        prev_mm = m_T.n
        prev_ms = m_S.n
        prev_mv = m_V.n

        def otile(kind, ot, idx):
            src = (d_wkv if kind == "k" else d_wq)[ot * D:(ot + 1) * D, :]
            wb = s_wblk[idx % 2]
            gate = ev.get(f"mm_{idx - 2}", prev_mm)
            b.g(lambda g, gate=gate: g.wait_ge(m_T.h, gate))
            lsw, lvw = Qg.dma(lambda g, wb=wb, src=src: g.dma_start(
                out=wb[:], in_=src.rearrange("(c p) o -> p c o", p=128)))
            b.t(lambda t, lsw=lsw, lvw=lvw: t.wait_ge(lsw.h, lvw))
            for tb in range(2):
                pp = p_pr[tb]
                gate = ev.get(f"ev_{tb}", prev_mv)
                b.t(lambda t, gate=gate: t.wait_ge(m_V.h, gate))
                ins = None
                for dd in range(NCH):
                    ins = b.t(lambda t, dd=dd, tb=tb, wb=wb, pp=pp: t.matmul(
                        pp[:], wb[:, dd, :], s_xnT[:, dd, tb * 512:(tb + 1) * 512],
                        start=(dd == 0), stop=(dd == NCH - 1)))
                m_T.inc(ins)
                tn = m_T.n
                b.s(lambda s, tn=tn: s.wait_ge(m_T.h, tn))
                m_S.inc(b.s(lambda s, pp=pp: s.activation(out=s_sq[:], in_=pp[:],
                                                          func=AF.Square)))
                sqn = m_S.n
                b.t(lambda t, sqn=sqn: t.wait_ge(m_S.h, sqn))
                psq = (p_sq_k if kind == "k" else p_sq_q)[tb]
                m_T.inc(b.t(lambda t, psq=psq, first=(ot == 0): t.matmul(
                    psq[:], s_ones[:, :], s_sq[:], start=first, stop=False)))
                col = slice(half * 1024 + tb * 512, half * 1024 + (tb + 1) * 512)
                sqn2 = m_S.n
                b.v(lambda v, sqn2=sqn2: v.wait_ge(m_S.h, sqn2))
                b.v(lambda v: v.drain())
                if kind == "q":
                    m_V.inc(b.v(lambda v, ot=ot, col=col, pp=pp: v.tensor_copy(
                        out=s_qF[:, ot, col], in_=pp[:])))
                else:
                    m_V.inc(b.v(lambda v, ot=ot, col=col, pp=pp: v.tensor_copy(
                        out=s_kF[2 * ot][0:64, col], in_=pp[0:64, :])))
                    m_V.inc(b.v(lambda v, ot=ot, col=col, pp=pp: v.tensor_copy(
                        out=s_kF[2 * ot + 1][64:128, col], in_=pp[64:128, :])))
                ev[f"ev_{tb}"] = m_V.n
            ev[f"mm_{idx}"] = m_T.n

        for ot in range(4):
            otile("k", ot, ot)
        for ot in range(16):
            otile("q", ot, 4 + ot)

        for psq in (p_sq_q[0], p_sq_q[1], p_sq_k[0], p_sq_k[1]):
            m_T.inc(b.t(lambda t, psq=psq: t.matmul(
                psq[:], s_zeros[0:1, 0:128], s_zeros[0:1, 0:512],
                start=False, stop=True)))
        tn_sq = m_T.n
        b.v(lambda v, tn_sq=tn_sq: v.wait_ge(m_T.h, tn_sq))

        for tb in range(2):
            col = slice(half * 1024 + tb * 512, half * 1024 + (tb + 1) * 512)
            rq = rsq512(p_sq_q[tb], D, EPS_QK, 0)
            for j in range(NCH):
                m_V.inc(b.v(lambda v, j=j, col=col, rq=rq: v.scalar_tensor_tensor(
                    out=s_qF[:, j, col], in0=s_qF[:, j, col],
                    scalar=s_qnw8[:, j:j + 1], in1=rq[:],
                    op0=ALU.mult, op1=ALU.mult)))
            rk = rsq512(p_sq_k[tb], 512, EPS_QK, 1)
            for kk in range(4):
                m_V.inc(b.v(lambda v, kk=kk, col=col, rk=rk: v.scalar_tensor_tensor(
                    out=s_kF[2 * kk][0:64, col], in0=s_kF[2 * kk][0:64, col],
                    scalar=s_knw[0:64, kk:kk + 1], in1=rk[0:64, :],
                    op0=ALU.mult, op1=ALU.mult)))
                m_V.inc(b.v(lambda v, kk=kk, col=col, rk=rk: v.scalar_tensor_tensor(
                    out=s_kF[2 * kk + 1][64:128, col],
                    in0=s_kF[2 * kk + 1][64:128, col],
                    scalar=s_knw[64:128, kk:kk + 1], in1=rk[64:128, :],
                    op0=ALU.mult, op1=ALU.mult)))
        vn = m_V.n
        b.s(lambda s, vn=vn: s.wait_ge(m_V.h, vn))
        for g in range(NKV):
            col = slice(half * 1024, half * 1024 + 1024)
            if g % 2 == 0:
                ks_, kv_ = Qs.dma(lambda s, g=g, col=col: s.dma_start(
                    out=s_kF[g][64:128, col], in_=s_kF[g][0:64, col]))
            else:
                ks_, kv_ = Qs.dma(lambda s, g=g, col=col: s.dma_start(
                    out=s_kF[g][0:64, col], in_=s_kF[g][64:128, col]))
            kdup_waits.append((ks_, kv_))

        for vot in range(4):
            wb = s_wblk[vot % 2]
            gate = ev.get(f"vmm_{vot - 2}", ev.get("mm_19", prev_mm))
            b.g(lambda g, gate=gate: g.wait_ge(m_T.h, gate))
            src = d_wkv[(4 + vot) * D:(5 + vot) * D, :]
            lsw, lvw = Qg.dma(lambda g, wb=wb, src=src: g.dma_start(
                out=wb[:], in_=src.rearrange("(c p) o -> p c o", p=128)))
            b.t(lambda t, lsw=lsw, lvw=lvw: t.wait_ge(lsw.h, lvw))
            for ti in range(8):
                tt = half * 8 + ti
                pp = p_pr[ti % 2]
                if ti >= 2:
                    gate = ev[f"vev_{vot}_{ti - 2}"]
                elif vot >= 1:
                    gate = ev[f"vev_{vot - 1}_{ti + 6}"]
                else:
                    gate = ev.get("ev_1", prev_mv)
                b.t(lambda t, gate=gate: t.wait_ge(m_V.h, gate))
                ins = None
                for dd in range(NCH):
                    ins = b.t(lambda t, dd=dd, ti=ti, pp=pp, wb=wb: t.matmul(
                        pp[:, 0:128], s_xnT[:, dd, ti * 128:(ti + 1) * 128],
                        wb[:, dd, :], start=(dd == 0), stop=(dd == NCH - 1)))
                m_T.inc(ins)
                tn = m_T.n
                b.v(lambda v, tn=tn: v.wait_ge(m_T.h, tn))
                b.v(lambda v: v.drain())
                b.v(lambda v, tt=tt, vot=vot, pp=pp: v.tensor_copy(
                    out=s_vF[:, tt, (2 * vot) * 65:(2 * vot) * 65 + 64],
                    in_=pp[:, 0:64]))
                m_V.inc(b.v(lambda v, tt=tt, vot=vot, pp=pp: v.tensor_copy(
                    out=s_vF[:, tt, (2 * vot + 1) * 65:(2 * vot + 1) * 65 + 64],
                    in_=pp[:, 64:128])))
                ev[f"vev_{vot}_{ti}"] = m_V.n
            ev[f"vmm_{vot}"] = m_T.n

    kdup_waits = []
    s_attnT = None


    def phase_B():
        pf, _ = psum_pool([512, 512, 512, 2048])
        ring = pf[0:3]
        pvo = pf[3]
        for em in (b.t, b.v, b.s):
            for ks_, kv_ in kdup_waits:
                em(lambda e, ks_=ks_, kv_=kv_: e.wait_ge(ks_.h, kv_))
        b.flush()
        scopes["SA"].close()
        NPT = 3
        s_PT = sb("s_PT", (128, NPT, WMAX), BF16, "SB")
        s_rsD = sb("s_rsD", (128, 1024), F32, "SB")
        s_ab = [sb(f"s_ab{i}", (128, WMAX), F32, "SB") for i in range(2)]
        nonlocal s_prep, s_prep8, s_attnT
        s_prep = [sb("s_prep0", (128, D), BF16, "SB")] * 2
        s_prep8 = [sb("s_prep8", (128, D), FP8, "SB")] * 2
        s_attnT = sb("s_attnT", (128, NCH, T), BF16, "SBC", side="right")
        ab_ld = {}
        ab_va = {}
        ab_ld[0] = Qy.dma(lambda y: y.dma_start(out=s_ab[0][:],
                                                in_=d_astrip[0:128, :]))

        slot_gate = [0, 0, 0]
        pt_gate = [0, 0, 0]
        prev_evac = 0
        sl = _slopes()

        for h in range(NH):
            g = h >> 2
            hb = h & 1
            j = h >> 1
            rows = slice(hb * 64, hb * 64 + 64)
            if h + 1 < NH:
                gt, gv = ab_va.get(h - 1, (0, 0))
                b.y(lambda y, gt=gt: y.wait_ge(m_T.h, gt))
                b.y(lambda y, gv=gv: y.wait_ge(m_V.h, gv))
                ab_ld[h + 1] = Qy.dma(lambda y, h=h: y.dma_start(
                    out=s_ab[(h + 1) % 2][:],
                    in_=d_astrip[(h + 1) * 128:(h + 2) * 128, :]))
            prep_load(4 * h)
            b.t(lambda t, pe=prev_evac: t.wait_ge(m_V.h, pe))
            for bb in range(4):
                ins = b.t(lambda t, bb=bb: t.matmul(
                    pvo[:, bb * 512:(bb + 1) * 512], s_zeros[0:1, 0:128],
                    s_zeros[0:1, 0:512], start=True, stop=True))
            m_T.inc(ins)
            flush_n = m_T.n
            sub_ctr = [0]
            chunk_exp = {}

            def emit_chunk(c, g=g, hb=hb, j=j, rows=rows, h=h):
                Wc = min(WMAX, T - c * 128)
                nsub = (Wc + 511) // 512
                slot = c % NPT
                exp_gate_pv = pt_gate[slot]
                exp_first = True
                for sref in range(nsub):
                    a0 = sref * 512
                    b0 = min(a0 + 512, Wc)
                    ws = b0 - a0
                    si = sub_ctr[0]
                    sub_ctr[0] += 1
                    rslot = ring[si % 3]
                    gate_s = slot_gate[si % 3]
                    b.t(lambda t, gate_s=gate_s: t.wait_ge(m_S.h, gate_s))
                    if c == 0 and sref == 0:
                        ls_ab, lv_ab = ab_ld[h]
                        b.t(lambda t, ls_ab=ls_ab, lv_ab=lv_ab: t.wait_ge(
                            ls_ab.h, lv_ab))
                    on_T = False
                    if on_T:
                        b.t(lambda t, rslot=rslot, c=c, a0=a0, ws=ws, g=g,
                            j=j, rows=rows: t.matmul(
                            rslot[:, 0:ws], s_kF[g][rows, c * 128:(c + 1) * 128],
                            s_qF[rows, j, c * 128 + a0: c * 128 + a0 + ws],
                            start=True, stop=False))
                        m_T.inc(b.t(lambda t, rslot=rslot, h=h, a0=a0,
                                    ws=ws: t.matmul(
                            rslot[:, 0:ws], s_idfb[:],
                            s_ab[h % 2][:, a0:a0 + ws],
                            start=False, stop=True)))
                        mm_n = m_T.n
                        b.s(lambda sc, mm_n=mm_n: sc.wait_ge(m_T.h, mm_n))
                    else:
                        m_T.inc(b.t(lambda t, rslot=rslot, c=c, a0=a0, ws=ws,
                                    g=g, j=j, rows=rows: t.matmul(
                            rslot[:, 0:ws], s_kF[g][rows, c * 128:(c + 1) * 128],
                            s_qF[rows, j, c * 128 + a0: c * 128 + a0 + ws],
                            start=True, stop=True)))
                        mm_n = m_T.n
                        b.v(lambda v, mm_n=mm_n: v.wait_ge(m_T.h, mm_n))
                        if c == 0:
                            ls_ab, lv_ab = ab_ld[h]
                            b.v(lambda v, ls_ab=ls_ab, lv_ab=lv_ab: v.wait_ge(
                                ls_ab.h, lv_ab))
                        m_V.inc(b.v(lambda v, rslot=rslot, h=h, a0=a0, ws=ws:
                                    v.tensor_add(
                            out=rslot[:, 0:ws], in0=rslot[:, 0:ws],
                            in1=s_ab[h % 2][:, a0:a0 + ws])))
                        va_n = m_V.n
                        b.s(lambda sc, va_n=va_n: sc.wait_ge(m_V.h, va_n))
                    if exp_first:
                        b.s(lambda sc, gp=exp_gate_pv: sc.wait_ge(m_pv.h, gp))
                        exp_first = False
                    ins = b.s(lambda sc, rslot=rslot, slot=slot, a0=a0, b0=b0,
                              ws=ws: sc.activation(
                        out=s_PT[:, slot, a0:b0],
                        in_=rslot[:, 0:ws], func=AF.Exp, scale=1.0))
                    m_S.inc(ins)
                    slot_gate[si % 3] = m_S.n
                chunk_exp[c] = m_S.n

            def emit_pv(c, g=g, hb=hb):
                Wc = min(WMAX, T - c * 128)
                slot = c % NPT
                en = chunk_exp[c]
                b.t(lambda t, en=en: t.wait_ge(m_S.h, en))
                gc0 = c * 128
                gc1 = c * 128 + Wc
                pieces = []
                p0 = gc0
                while p0 < gc1:
                    p1 = min((p0 // 512 + 1) * 512, gc1)
                    pieces.append((p0, p1))
                    p0 = p1
                ins = None
                for (p0, p1) in pieces:
                    w0 = p0 - gc0
                    ins = b.t(lambda t, w0=w0, p0=p0, p1=p1, c=c, g=g,
                              slot=slot: t.matmul(
                        pvo[0:65, p0:p1],
                        s_vF[:, c, g * 65:(g + 1) * 65],
                        s_PT[:, slot, w0:w0 + (p1 - p0)],
                        start=False, stop=False,
                        skip_group_check=True))
                m_pv.inc(ins)
                pt_gate[slot] = m_pv.n

            emit_chunk(0)
            for c in range(16):
                if c + 1 < 16:
                    emit_chunk(c + 1)
                emit_pv(c)
                if c in (3, 6, 9, 12):
                    kk = c // 3 - 1
                    prep_fin(4 * h + kk)
                    if kk < 3:
                        prep_load(4 * h + kk + 1)
            ab_va[h] = (m_T.n, m_V.n)
            pvn = m_pv.n
            drow = slice(64, 65)
            orow = slice(0, 64)
            orow_at = slice(hb * 64, hb * 64 + 64)
            b.s(lambda sc, pvn=pvn: sc.wait_ge(m_pv.h, pvn))
            b.s(lambda sc: sc.drain())
            ln_done = {}
            m_S.inc(b.s(lambda sc, drow=drow: sc.activation(
                out=s_rsD[64:65, 0:1024], in_=pvo[drow, 0:1024], func=AF.Ln)))
            ln_done[0] = m_S.n
            tp = (64, 0)
            exp_done = {}
            stt_done = {}

            def bcast(bb, orow=orow, tp=tp, drow=drow):
                rslot = ring[bb % 3]
                gate_s = slot_gate[bb % 3]
                if bb == 2 and 1 not in ln_done:
                    btn = m_T.n
                    b.s(lambda sc, btn=btn: sc.wait_ge(m_T.h, btn))
                    m_S.inc(b.s(lambda sc, drow=drow: sc.activation(
                        out=s_rsD[64:65, 0:1024], in_=pvo[drow, 1024:2048], func=AF.Ln)))
                    ln_done[1] = m_S.n
                b.t(lambda t, lnn=ln_done[min(bb // 2, 1)]: t.wait_ge(m_S.h, lnn))
                b.t(lambda t, gate_s=gate_s: t.wait_ge(m_S.h, gate_s))
                if bb == 3:
                    b.t(lambda t, ex=exp_done[0]: t.wait_ge(m_S.h, ex))
                m_T.inc(b.t(lambda t, rslot=rslot, bb=bb, orow=orow, tp=tp: t.matmul(
                    rslot[orow, :], s_onesf[64:65, 0:64],
                    s_rsD[64:65, (bb % 2) * 512 + 0:(bb % 2) * 512 + 512],
                    start=True, stop=True, tile_position=tp)))
                return m_T.n

            def expev(bb, bcn, orow=orow, j=j):
                rslot = ring[bb % 3]
                b.s(lambda sc, bcn=bcn: sc.wait_ge(m_T.h, bcn))
                if bb >= 2:
                    b.s(lambda sc, g2=stt_done[bb - 2]: sc.wait_ge(m_V.h, g2))
                m_S.inc(b.s(lambda sc, rslot=rslot, bb=bb, orow=orow: sc.activation(
                    out=s_rsD[orow, (bb % 2) * 512:(bb % 2 + 1) * 512],
                    in_=rslot[orow, :], func=AF.Exp, scale=-1.0)))
                slot_gate[bb % 3] = m_S.n
                exp_done[bb] = m_S.n
                rdn = m_S.n
                b.v(lambda v, rdn=rdn: v.wait_ge(m_S.h, rdn))
                m_V.inc(b.v(lambda v, bb=bb, j=j, orow=orow,
                            orow_at=orow_at: v.scalar_tensor_tensor(
                    out=s_attnT[orow_at, j, bb * 512:(bb + 1) * 512],
                    in0=pvo[orow, bb * 512:(bb + 1) * 512], scalar=1.0,
                    in1=s_rsD[orow, (bb % 2) * 512:(bb % 2 + 1) * 512],
                    op0=ALU.mult, op1=ALU.mult)))
                stt_done[bb] = m_V.n

            bc = [bcast(0), bcast(1), bcast(2)]
            expev(0, bc[0])
            expev(1, bc[1])
            bc.append(bcast(3))
            expev(2, bc[2])
            expev(3, bc[3])
            prev_evac = m_V.n
        b.t(lambda t, pe=prev_evac: t.wait_ge(m_V.h, pe))
        for bb in range(4):
            m_T.inc(b.t(lambda t, bb=bb: t.matmul(
                pvo[:, bb * 512:(bb + 1) * 512], s_zeros[0:1, 0:128],
                s_zeros[0:1, 0:512], start=True, stop=True)))
        prep_finalize()
        return prev_evac


    C1 = float(np.sqrt(D))
    C2 = float(np.sqrt(INNER))

    def phase_C():
        pf, _ = psum_pool(8)
        for em in (b.y, b.g, b.s):
            for Q in (Qs, Qg, Qy):
                Q.wait_all(em)
        b.flush()
        scopes["SB"].close()
        scopes["SQKV"].close()

        s_woF = sb("s_woF", (128, NCH, D), BF16, "SCW", side="right")
        s_b2B = sb("s_b2B", (128, D), F32, "SCW")
        s_tm = [sb(f"s_tm{i}", (128, D), F32, "SCW") for i in range(2)]
        s_xw = [sb(f"s_xw{i}", (128, D), F32, "SCW") for i in range(2)]

        ev_b = 0
        row_t = 0
        for i in range(D // 512):
            b.y(lambda y, g=row_t: y.wait_ge(m_T.h, g))
            ls_, lv_ = Qy.dma(lambda y, i=i: y.dma_start(
                out=s_tm[0][0:1, 0:512], in_=d_b2[0:1, i * 512:(i + 1) * 512]))
            pp = pf[i % 2]
            b.t(lambda t, ls_=ls_, lv_=lv_: t.wait_ge(ls_.h, lv_))
            if ev_b:
                b.t(lambda t, ev_b=ev_b: t.wait_ge(m_V.h, ev_b))
            m_T.inc(b.t(lambda t, pp=pp: t.matmul(
                pp[:], s_onesf[0:1, :], s_tm[0][0:1, 0:512], start=True, stop=True)))
            row_t = m_T.n
            b.v(lambda v, tn=row_t: v.wait_ge(m_T.h, tn))
            b.v(lambda v: v.drain())
            m_V.inc(b.v(lambda v, pp=pp, i=i: v.tensor_copy(
                out=s_b2B[:, i * 512:(i + 1) * 512], in_=pp[:])))
            ev_b = m_V.n
        b2_v = m_V.n
        b2_t = m_T.n

        wo_ld = []
        for dH in range(NCH):
            wo_ld.append(Qg.dma(lambda g, dH=dH: g.dma_start(
                out=s_woF[:, dH, :], in_=d_wo[dH * 128:(dH + 1) * 128, :])))

        wo_ev = {}
        wo_st = {}
        for tt in range(16):
            bs = (tt % 2) * 4
            xb = s_xw[tt % 2]
            tb = s_tm[tt % 2]
            if tt >= 2:
                gate = wo_ev[tt - 2]
                b.y(lambda y, gate=gate: y.wait_ge(m_V.h, gate))
            lsx, lvx = Qy.dma(lambda y, xb=xb, tt=tt: y.dma_start(
                out=xb[:], in_=d_x[tt * 128:(tt + 1) * 128, :]))
            if tt >= 2:
                gate = wo_ev[tt - 2]
                b.t(lambda t, gate=gate: t.wait_ge(m_V.h, gate))
            if tt == 0:
                b.t(lambda t, g=b2_t: t.wait_ge(m_T.h, g))
                b.t(lambda t, g=b2_v: t.wait_ge(m_V.h, g))
            ins = None
            for dH in range(NCH):
                if tt == 0:
                    ls_, lv_ = wo_ld[dH]
                    b.t(lambda t, ls_=ls_, lv_=lv_: t.wait_ge(ls_.h, lv_))
                for bb in range(4):
                    ins = b.t(lambda t, bb=bb, dH=dH, tt=tt, bs=bs: t.matmul(
                        pf[bs + bb][:],
                        s_attnT[:, dH, tt * 128:(tt + 1) * 128],
                        s_woF[:, dH, bb * 512:(bb + 1) * 512],
                        start=(dH == 0), stop=(dH == NCH - 1)))
            m_T.inc(ins)
            won = m_T.n
            b.v(lambda v, lsx=lsx, lvx=lvx: v.wait_ge(lsx.h, lvx))
            if tt >= 2:
                gs_, gv_ = wo_st[tt - 2]
                b.v(lambda v, gs_=gs_, gv_=gv_: v.wait_ge(gs_.h, gv_))
            b.v(lambda v, won=won: v.wait_ge(m_T.h, won))
            if tt == 0:
                b.v(lambda v, g=b2_v: v.wait_ge(m_V.h, g))
            for bb in range(4):
                b.v(lambda v, bb=bb, xb=xb, tb=tb, bs=bs: v.scalar_tensor_tensor(
                    out=tb[:, bb * 512:(bb + 1) * 512],
                    in0=pf[bs + bb][:], scalar=1.0,
                    in1=xb[:, bb * 512:(bb + 1) * 512],
                    op0=ALU.mult, op1=ALU.add))
            b.v(lambda v: v.drain())
            ins = None
            for bb in range(4):
                ins = b.v(lambda v, bb=bb, tb=tb: v.tensor_tensor(
                    out=tb[:, bb * 512:(bb + 1) * 512],
                    in0=tb[:, bb * 512:(bb + 1) * 512],
                    in1=s_b2B[:, bb * 512:(bb + 1) * 512], op=ALU.add))
            m_V.inc(ins)
            wo_ev[tt] = m_V.n
            ev = m_V.n
            b.s(lambda sc, ev=ev: sc.wait_ge(m_V.h, ev))
            wo_st[tt] = Qs.dma(lambda sc, tb=tb, tt=tt: sc.dma_start(
                out=d_tmp2[tt * 128:(tt + 1) * 128, :], in_=tb[:]))

        pf, _ = psum_pool(8)
        for em in (b.y, b.g, b.s, b.v, b.t):
            Qs.wait_all(em)
            Qg.wait_all(em)
            Qy.wait_all(em)
        b.flush()
        scopes["SCW"].close()
        scopes["SBC"].close()
        pfT = pf[4:8]

        s_H = [sb(f"s_H{i}", (128, INNER), F32, "SC") for i in range(2)]
        s_b1B = sb("s_b1B", (128, INNER), BF16, "SC")
        s_x1T = [sb(f"s_x1T{i}", (128, NCH, 128), BF16, "SC") for i in range(2)]
        s_x2T = sb("s_x2T", (128, 64, 128), BF16, "SC")
        s_w1b = [sb(f"s_w1b{i}", (128, 4, 1024), FP8, "SC") for i in range(4)]
        s_w2b = [sb(f"s_w2b{i}", (128, 2, D), FP8, "SC") for i in range(4)]
        s_xc = [sb(f"s_xc{i}", (128, D), F32, "SC") for i in range(2)]
        s_t2 = [sb(f"s_t2{i}", (128, D), F32, "SC") for i in range(2)]
        s_ot = [sb(f"s_ot{i}", (128, D), F32, "SC") for i in range(2)]
        s_fc = [sb(f"s_fc{i}", (128, 24), F32, "SC") for i in range(2)]
        s_qf = sb("s_qf", (128, 112), F32, "SC")
        s_idf = sb("s_idf", (128, 128), F32, "SC")
        idf_ls, idf_lv = Qy.dma(lambda y: y.dma_start(out=s_idf[:], in_=d_identf))

        def rsqrt_wide(tsr, src, dst, w, scr):
            b.v(lambda v: v.reciprocal(out=tsr[:, scr:scr + w],
                                       in_=tsr[:, src:src + w]))
            b.v(lambda v: v.drain())
            m_V.inc(b.v(lambda v: v.tensor_copy(out=tsr[:, scr + w:scr + w + 1],
                                                in_=tsr[:, scr:scr + 1])))
            vv = m_V.n
            b.s(lambda sc, vv=vv: sc.wait_ge(m_V.h, vv))
            m_S.inc(b.s(lambda sc: sc.activation(out=tsr[:, dst:dst + w],
                                                 in_=tsr[:, scr:scr + w],
                                                 func=AF.Sqrt)))
            ss = m_S.n
            b.v(lambda v, ss=ss: v.wait_ge(m_S.h, ss))
            b.v(lambda v: v.tensor_mul(out=tsr[:, scr:scr + w],
                                       in0=tsr[:, dst:dst + w],
                                       in1=tsr[:, dst:dst + w]))
            b.v(lambda v: v.drain())
            b.v(lambda v: v.tensor_mul(out=tsr[:, scr:scr + w],
                                       in0=tsr[:, scr:scr + w],
                                       in1=tsr[:, src:src + w]))
            b.v(lambda v: v.drain())
            b.v(lambda v: v.tensor_scalar(out=tsr[:, scr:scr + w],
                                          in0=tsr[:, scr:scr + w],
                                          scalar1=-0.5, scalar2=1.5,
                                          op0=ALU.mult, op1=ALU.add))
            b.v(lambda v: v.drain())
            b.v(lambda v: v.tensor_mul(out=tsr[:, dst:dst + w],
                                       in0=tsr[:, dst:dst + w],
                                       in1=tsr[:, scr:scr + w]))
            b.v(lambda v: v.drain())

        b.v(lambda v: v.tensor_mul(out=s_qf[:, 48:64], in0=s_lnv[:], in1=s_lnr[:]))
        b.v(lambda v: v.drain())
        b.v(lambda v: v.tensor_mul(out=s_qf[:, 48:64], in0=s_qf[:, 48:64],
                                   in1=s_lnr[:]))
        b.v(lambda v: v.drain())
        b.v(lambda v: v.tensor_scalar(out=s_qf[:, 48:64], in0=s_qf[:, 48:64],
                                      scalar1=1.0, scalar2=EPS_RMS,
                                      op0=ALU.mult, op1=ALU.add))
        b.v(lambda v: v.drain())
        rsqrt_wide(s_qf, 48, 64, 16, 80)
        b.v(lambda v: v.tensor_mul(out=s_qf[:, 80:96], in0=s_qf[:, 48:64],
                                   in1=s_qf[:, 64:80]))
        b.v(lambda v: v.drain())
        b.v(lambda v: v.tensor_scalar_mul(out=s_qf[:, 48:64], in0=s_qf[:, 80:96],
                                          scalar1=EPS_Q * C1))
        b.v(lambda v: v.drain())
        b.v(lambda v: v.tensor_max(out=s_qf[:, 48:64], in0=s_qf[:, 48:64],
                                   in1=s_mx1[:]))
        b.v(lambda v: v.drain())
        b.v(lambda v: v.tensor_mul(out=s_qf[:, 32:48], in0=s_qf[:, 48:64],
                                   in1=s_qf[:, 64:80]))
        b.v(lambda v: v.drain())
        b.v(lambda v: v.tensor_scalar(out=s_qf[:, 32:48], in0=s_qf[:, 32:48],
                                      scalar1=s_pw[:, 2:3],
                                      scalar2=1.0 / (127.0 * C1),
                                      op0=ALU.mult, op1=ALU.mult))
        b.v(lambda v: v.drain())
        b.v(lambda v: v.reciprocal(out=s_qf[:, 80:96], in_=s_qf[:, 48:64]))
        b.v(lambda v: v.drain())
        b.v(lambda v: v.tensor_scalar_mul(out=s_qf[:, 80:96], in0=s_qf[:, 80:96],
                                          scalar1=127.0))
        b.v(lambda v: v.drain())
        b.v(lambda v: v.tensor_mul(out=s_qf[:, 0:16], in0=s_lnr[:],
                                   in1=s_qf[:, 80:96]))
        b.v(lambda v: v.drain())
        b.v(lambda v: v.tensor_mul(out=s_qf[:, 16:32], in0=s_lnm[:],
                                   in1=s_qf[:, 0:16]))
        b.v(lambda v: v.drain())
        m_V.inc(b.v(lambda v: v.tensor_scalar_mul(out=s_qf[:, 16:32],
                                                  in0=s_qf[:, 16:32],
                                                  scalar1=-1.0)))
        qf_v = m_V.n

        ev_b = 0
        row_t = 0
        for i in range(INNER // 512):
            b.y(lambda y, g=row_t: y.wait_ge(m_T.h, g))
            ls_, lv_ = Qy.dma(lambda y, i=i: y.dma_start(
                out=s_ot[0][0:1, 0:512], in_=d_b1[0:1, i * 512:(i + 1) * 512]))
            pp = pf[i % 2]
            b.t(lambda t, ls_=ls_, lv_=lv_: t.wait_ge(ls_.h, lv_))
            if ev_b:
                b.t(lambda t, ev_b=ev_b: t.wait_ge(m_V.h, ev_b))
            m_T.inc(b.t(lambda t, pp=pp: t.matmul(
                pp[:], s_onesf[0:1, :], s_ot[0][0:1, 0:512],
                start=True, stop=True)))
            row_t = m_T.n
            b.v(lambda v, tn=row_t: v.wait_ge(m_T.h, tn))
            b.v(lambda v: v.drain())
            m_V.inc(b.v(lambda v, pp=pp, i=i: v.tensor_copy(
                out=s_b1B[:, i * 512:(i + 1) * 512], in_=pp[:])))
            ev_b = m_V.n
        b1_v = m_V.n
        b1_t = m_T.n

        def transpose_f32(src_tensor, nch, dst_fn):
            for bt in range(nch // 4):
                gate = m_V.n
                b.t(lambda t, gate=gate: t.wait_ge(m_V.h, gate))
                ins = None
                for kk in range(4):
                    cc = bt * 4 + kk
                    ins = b.t(lambda t, kk=kk, cc=cc,
                              src_tensor=src_tensor: t.transpose(
                        pfT[kk][:, 0:128],
                        src_tensor[:, cc * 128:(cc + 1) * 128], s_idf[:]))
                m_T.inc(ins)
                trn = m_T.n
                b.v(lambda v, trn=trn: v.wait_ge(m_T.h, trn))
                b.v(lambda v: v.drain())
                ins = None
                for kk in range(4):
                    cc = bt * 4 + kk
                    ins = b.v(lambda v, kk=kk, cc=cc: dst_fn(v, cc,
                                                            pfT[kk][:, 0:128]))
                m_V.inc(ins)

        tr1_t, vm_v, x1t_v, q1_s = {}, {}, {}, {}
        f1mm, f1ev = {}, {}
        tr2_t, x2t_v, f2mm, f2_t = {}, {}, {}, {}
        vmag_v, epi_v, ssq_s, st_d, t2_ld = {}, {}, {}, {}, {}

        x_ld = {}

        def x_load(k):
            xb = s_xc[k % 2]
            gate = tr1_t.get(k - 2, 0)
            b.y(lambda y, gate=gate: y.wait_ge(m_T.h, gate))
            x_ld[k] = Qy.dma(lambda y, xb=xb, k=k: y.dma_start(
                out=xb[:], in_=d_x[k * 128:(k + 1) * 128, :]))

        def stage_q1(k):
            xb = s_xc[k % 2]
            if k == 0:
                x_load(0)
            if k + 1 < 16:
                x_load(k + 1)
            lsx, lvx = x_ld[k]
            b.s(lambda sc, lsx=lsx, lvx=lvx: sc.wait_ge(lsx.h, lvx))
            if k == 0:
                b.s(lambda sc, g=qf_v: sc.wait_ge(m_V.h, g))
            b.s(lambda sc: sc.drain())
            m_S.inc(b.s(lambda sc, xb=xb, k=k: sc.activation(
                out=xb[:], in_=xb[:], func=AF.Identity,
                bias=s_qf[:, 16 + k:17 + k], scale=s_qf[:, k:k + 1])))
            b.s(lambda sc: sc.drain())
            m_S.inc(b.s(lambda sc, xb=xb: sc.activation(
                out=xb[:], in_=xb[:], func=AF.Identity,
                bias=s_mgc[:, 0:1], scale=1.0)))
            b.s(lambda sc: sc.drain())
            m_S.inc(b.s(lambda sc, xb=xb: sc.activation(
                out=xb[:], in_=xb[:], func=AF.Identity,
                bias=s_mgcn[:, 0:1], scale=1.0)))
            q1_s[k] = m_S.n
            vm_v[k] = m_S.n
            tgate = vm_v[k]
            b.t(lambda t, tgate=tgate: t.wait_ge(m_S.h, tgate))
            egate = epi_v.get(k - 3, 0)
            b.t(lambda t, egate=egate: t.wait_ge(m_V.h, egate))
            if k == 0:
                b.t(lambda t, ls_=idf_ls, lv_=idf_lv: t.wait_ge(ls_.h, lv_))
            transpose_f32(s_xc[k % 2], NCH,
                          lambda v, cc, src, k=k: v.tensor_copy(
                              out=s_x1T[k % 2][:, cc, :], in_=src))
            tr1_t[k] = m_T.n
            x1t_v[k] = m_V.n

        def stage_f1(k):
            for ib in range(8):
                for d4 in range(4):
                    k4 = ib * 4 + d4
                    wb1 = s_w1b[k4 % 4]
                    gate = f1mm.get((k, k4 - 4), f1mm.get((k - 1, k4 + 28), 0))
                    if k4 % 2 == 0:
                        b.y(lambda y, gate=gate: y.wait_ge(m_T.h, gate))
                        ls_, lv_ = Qy.dma(lambda y, wb1=wb1, k4=k4: y.dma_start(
                            out=wb1[:], in_=d_w1s[k4 * 512:(k4 + 1) * 512, :]
                            .rearrange("(c p) i -> p c i", p=128)))
                    else:
                        b.s(lambda sc, gate=gate: sc.wait_ge(m_T.h, gate))
                        ls_, lv_ = Qs.dma(lambda sc, wb1=wb1, k4=k4: sc.dma_start(
                            out=wb1[:], in_=d_w1s[k4 * 512:(k4 + 1) * 512, :]
                            .rearrange("(c p) i -> p c i", p=128)))
                    b.t(lambda t, ls_=ls_, lv_=lv_: t.wait_ge(ls_.h, lv_))
                    if d4 == 0:
                        gate = f1ev.get((k, ib - 2), f1ev.get((k - 1, ib + 6), 0))
                        b.t(lambda t, gate=gate: t.wait_ge(m_V.h, gate))
                        if ib == 0:
                            b.t(lambda t, g=x1t_v[k]: t.wait_ge(m_V.h, g))
                            if k == 0:
                                b.t(lambda t, g=b1_v: t.wait_ge(m_V.h, g))
                    ins = None
                    for sub in range(4):
                        dd = d4 * 4 + sub
                        for hh in range(2):
                            ins = b.t(lambda t, ib=ib, dd=dd, sub=sub, hh=hh,
                                      wb1=wb1, k=k: t.matmul(
                                pf[(ib % 2) * 2 + hh][:],
                                s_x1T[k % 2][:, dd, :],
                                wb1[:, sub, hh * 512:(hh + 1) * 512],
                                start=(dd == 0), stop=(dd == NCH - 1)))
                    m_T.inc(ins)
                    f1mm[(k, k4)] = m_T.n
                f1n = m_T.n
                b.v(lambda v, f1n=f1n: v.wait_ge(m_T.h, f1n))
                ins = None
                for hh in range(2):
                    ins = b.v(lambda v, ib=ib, hh=hh, k=k: v.scalar_tensor_tensor(
                        out=s_H[k % 2][:, ib * 1024 + hh * 512:
                                       ib * 1024 + (hh + 1) * 512],
                        in0=pf[(ib % 2) * 2 + hh][:],
                        scalar=s_qf[:, 32 + k:33 + k],
                        in1=s_b1B[:, ib * 1024 + hh * 512:
                                  ib * 1024 + (hh + 1) * 512],
                        op0=ALU.mult, op1=ALU.add))
                m_V.inc(ins)
                f1ev[(k, ib)] = m_V.n

        def rsqrt_col(fc, src, dst):
            b.v(lambda v: v.reciprocal(out=fc[:, 20:21], in_=fc[:, src:src + 1]))
            b.v(lambda v: v.drain())
            m_V.inc(b.v(lambda v: v.tensor_copy(out=fc[:, 21:22],
                                                in_=fc[:, 20:21])))
            vv = m_V.n
            b.s(lambda sc, vv=vv: sc.wait_ge(m_V.h, vv))
            m_S.inc(b.s(lambda sc: sc.activation(out=fc[:, dst:dst + 1],
                                                 in_=fc[:, 20:21],
                                                 func=AF.Sqrt)))
            ss = m_S.n
            b.v(lambda v, ss=ss: v.wait_ge(m_S.h, ss))
            b.v(lambda v: v.tensor_mul(out=fc[:, 20:21], in0=fc[:, dst:dst + 1],
                                       in1=fc[:, dst:dst + 1]))
            b.v(lambda v: v.drain())
            b.v(lambda v: v.tensor_mul(out=fc[:, 20:21], in0=fc[:, 20:21],
                                       in1=fc[:, src:src + 1]))
            b.v(lambda v: v.drain())
            b.v(lambda v: v.tensor_scalar(out=fc[:, 20:21], in0=fc[:, 20:21],
                                          scalar1=-0.5, scalar2=1.5,
                                          op0=ALU.mult, op1=ALU.add))
            b.v(lambda v: v.drain())
            b.v(lambda v: v.tensor_mul(out=fc[:, dst:dst + 1],
                                       in0=fc[:, dst:dst + 1],
                                       in1=fc[:, 20:21]))
            b.v(lambda v: v.drain())

        def stage_mid(k):
            H = s_H[k % 2]
            fc = s_fc[k % 2]
            jk = s_ot[k % 2]
            lastev = f1ev[(k, 7)]
            b.s(lambda sc, g=lastev: sc.wait_ge(m_V.h, g))
            b.s(lambda sc: sc.drain())
            ins = None
            for ib in range(8):
                ins = b.s(lambda sc, ib=ib, H=H: sc.activation(
                    out=H[:, ib * 1024:(ib + 1) * 1024],
                    in_=H[:, ib * 1024:(ib + 1) * 1024], func=AF.Gelu))
            m_S.inc(ins)
            gel = m_S.n
            b.v(lambda v, gel=gel: v.wait_ge(m_S.h, gel))
            m_V.inc(b.v(lambda v, H=H, fc=fc: v.reduce_max(
                out=fc[:, 6:7], in_=H[:], axis=AX.X,
                apply_absolute_value=True)))
            if (k - 2) in st_d:
                gs_, gv_ = st_d[k - 2]
                b.s(lambda sc, gs_=gs_, gv_=gv_: sc.wait_ge(gs_.h, gv_))
            if k == 0:
                b.s(lambda sc, g=b1_t: sc.wait_ge(m_T.h, g))
            ins = None
            for q4 in range(4):
                b.s(lambda sc: sc.drain())
                ins = b.s(lambda sc, q4=q4, H=H, jk=jk, fc=fc: sc.activation(
                    out=jk[:],
                    in_=H[:, q4 * 2048:(q4 + 1) * 2048], func=AF.Square,
                    accum_out=fc[:, 8 + q4:9 + q4]))
            m_S.inc(ins)
            ssq_s[k] = m_S.n
            ssqn = m_S.n
            b.v(lambda v, ssqn=ssqn: v.wait_ge(m_S.h, ssqn))
            b.v(lambda v, fc=fc: v.tensor_add(out=fc[:, 8:9], in0=fc[:, 8:9],
                                              in1=fc[:, 9:10]))
            b.v(lambda v: v.drain())
            b.v(lambda v, fc=fc: v.tensor_add(out=fc[:, 10:11], in0=fc[:, 10:11],
                                              in1=fc[:, 11:12]))
            b.v(lambda v: v.drain())
            b.v(lambda v, fc=fc: v.tensor_add(out=fc[:, 8:9], in0=fc[:, 8:9],
                                              in1=fc[:, 10:11]))
            b.v(lambda v: v.drain())
            b.v(lambda v, fc=fc: v.tensor_scalar(out=fc[:, 0:1], in0=fc[:, 8:9],
                                                 scalar1=1.0 / INNER,
                                                 scalar2=EPS_RMS,
                                                 op0=ALU.mult, op1=ALU.add))
            b.v(lambda v: v.drain())
            rsqrt_col(fc, 0, 1)
            b.v(lambda v, fc=fc: v.tensor_mul(out=fc[:, 2:3], in0=fc[:, 0:1],
                                              in1=fc[:, 1:2]))
            b.v(lambda v: v.drain())
            b.v(lambda v, fc=fc: v.tensor_scalar_mul(out=fc[:, 3:4],
                                                     in0=fc[:, 2:3],
                                                     scalar1=EPS_Q * C2))
            b.v(lambda v: v.drain())
            b.v(lambda v, fc=fc: v.tensor_max(out=fc[:, 3:4], in0=fc[:, 3:4],
                                              in1=fc[:, 6:7]))
            b.v(lambda v: v.drain())
            b.v(lambda v, fc=fc: v.reciprocal(out=fc[:, 4:5], in_=fc[:, 3:4]))
            b.v(lambda v: v.drain())
            b.v(lambda v, fc=fc: v.tensor_scalar_mul(out=fc[:, 4:5],
                                                     in0=fc[:, 4:5],
                                                     scalar1=127.0))
            b.v(lambda v: v.drain())
            b.v(lambda v, fc=fc: v.tensor_mul(out=fc[:, 5:6], in0=fc[:, 3:4],
                                              in1=fc[:, 1:2]))
            b.v(lambda v: v.drain())
            b.v(lambda v, fc=fc: v.tensor_mul(out=fc[:, 5:6], in0=fc[:, 5:6],
                                              in1=s_pw[:, 3:4]))
            b.v(lambda v: v.drain())
            m_V.inc(b.v(lambda v, fc=fc: v.tensor_scalar_mul(
                out=fc[:, 5:6], in0=fc[:, 5:6],
                scalar1=1.0 / (127.0 * C2))))
            vq2 = m_V.n
            b.s(lambda sc, vq2=vq2: sc.wait_ge(m_V.h, vq2))
            b.s(lambda sc: sc.drain())
            m_S.inc(b.s(lambda sc, H=H, fc=fc: sc.activation(
                out=H[:], in_=H[:], func=AF.Identity,
                bias=s_mgc[:, 0:1], scale=fc[:, 4:5])))
            b.s(lambda sc: sc.drain())
            m_S.inc(b.s(lambda sc, H=H: sc.activation(
                out=H[:], in_=H[:], func=AF.Identity,
                bias=s_mgcn[:, 0:1], scale=1.0)))
            vmag_v[k] = m_S.n

        def stage_f2(k):
            gate = epi_v.get(k - 2, 0)
            b.y(lambda y, gate=gate: y.wait_ge(m_V.h, gate))
            t2_ld[k] = Qy.dma(lambda y, k=k: y.dma_start(
                out=s_t2[k % 2][:], in_=d_tmp2[k * 128:(k + 1) * 128, :]))
            g1 = vmag_v[k]
            b.t(lambda t, g1=g1: t.wait_ge(m_S.h, g1))
            transpose_f32(s_H[k % 2], 64,
                          lambda v, cc, src: v.tensor_copy(out=s_x2T[:, cc, :],
                                                           in_=src))
            tr2_t[k] = m_T.n
            x2t_v[k] = m_V.n
            b.t(lambda t, g=x2t_v[k]: t.wait_ge(m_V.h, g))
            for ic2 in range(32):
                wb2 = s_w2b[ic2 % 4]
                gate = f2mm.get((k, ic2 - 4), f2mm.get((k - 1, ic2 + 28), 0))
                if ic2 < 4:
                    b.s(lambda sc, gate=gate: sc.wait_ge(m_T.h, gate))
                    ls_, lv_ = Qs.dma(lambda sc, wb2=wb2, ic2=ic2: sc.dma_start(
                        out=wb2[:], in_=d_w2s[ic2 * 256:(ic2 + 1) * 256, :]
                        .rearrange("(c p) o -> p c o", p=128)))
                else:
                    b.g(lambda g, gate=gate: g.wait_ge(m_T.h, gate))
                    ls_, lv_ = Qg.dma(lambda g, wb2=wb2, ic2=ic2: g.dma_start(
                        out=wb2[:], in_=d_w2s[ic2 * 256:(ic2 + 1) * 256, :]
                        .rearrange("(c p) o -> p c o", p=128)))
                b.t(lambda t, ls_=ls_, lv_=lv_: t.wait_ge(ls_.h, lv_))
                ins = None
                for sub in range(2):
                    ic = ic2 * 2 + sub
                    for bb in range(4):
                        ins = b.t(lambda t, bb=bb, ic=ic, sub=sub,
                                  wb2=wb2: t.matmul(
                            pf[4 + bb][:], s_x2T[:, ic, :],
                            wb2[:, sub, bb * 512:(bb + 1) * 512],
                            start=(ic == 0), stop=(ic == 63)))
                m_T.inc(ins)
                f2mm[(k, ic2)] = m_T.n
            f2_t[k] = m_T.n
            lst, lvt = t2_ld[k]
            b.v(lambda v, lst=lst, lvt=lvt: v.wait_ge(lst.h, lvt))
            if (k - 2) in st_d:
                gs_, gv_ = st_d[k - 2]
                b.v(lambda v, gs_=gs_, gv_=gv_: v.wait_ge(gs_.h, gv_))
            fn = f2_t[k]
            b.v(lambda v, fn=fn: v.wait_ge(m_T.h, fn))
            ins = None
            for bb in range(4):
                ins = b.v(lambda v, bb=bb, k=k: v.scalar_tensor_tensor(
                    out=s_ot[k % 2][:, bb * 512:(bb + 1) * 512],
                    in0=pf[4 + bb][:], scalar=s_fc[k % 2][:, 5:6],
                    in1=s_t2[k % 2][:, bb * 512:(bb + 1) * 512],
                    op0=ALU.mult, op1=ALU.add))
            m_V.inc(ins)
            epi_v[k] = m_V.n

        def emit_store(k):
            ev = epi_v[k]
            b.s(lambda sc, ev=ev: sc.wait_ge(m_V.h, ev))
            st_d[k] = Qst.dma(lambda sc, k=k: sc.dma_start(
                out=d_out[k * 128:(k + 1) * 128, :], in_=s_ot[k % 2][:]))

        stage_q1(0)
        stage_q1(1)
        stage_f1(0)
        for k in range(16):
            stage_mid(k)
            if k + 1 < 16:
                stage_f1(k + 1)
            if k + 2 < 16:
                stage_q1(k + 2)
            if k - 1 >= 0:
                emit_store(k - 1)
            stage_f2(k)
        emit_store(15)

    if "A" in stages:
        pfA, ptrbA = psum_pool(6, 1)
        phase_A_half(0, ptrbA[0])
        proj_half(0, pfA)
        phase_A_half(1, ptrbA[0])
        proj_half(1, pfA)

    if "B" in stages:
        phase_B()

    if "C" in stages:
        phase_C()

    if "attn" in dbg:
        o_a = dbg_out("attnT", (128, NCH * T), BF16)
        vn2, sn2 = m_V.n, m_S.n
        b.y(lambda y, vn2=vn2: y.wait_ge(m_V.h, vn2))
        b.y(lambda y, sn2=sn2: y.wait_ge(m_S.h, sn2))
        Qst.dma(lambda y: y.dma_start(out=o_a,
                                      in_=s_attnT[:].rearrange("p c t -> p (c t)")))

    if "qkv" in dbg:
        o_q = dbg_out("qF", (128, NCH * T), BF16)
        o_k = dbg_out("kF", (NKV * 128, T), BF16)
        o_v = dbg_out("vF", (128, NCH * NKV * (HD + 1)), BF16)
        o_ln = dbg_out("ln", (128, 4 * NCH))
        vn, sn = m_V.n, m_S.n
        b.y(lambda y, vn=vn: y.wait_ge(m_V.h, vn))
        b.y(lambda y, sn=sn: y.wait_ge(m_S.h, sn))
        for ks_, kv_ in kdup_waits:
            b.y(lambda y, ks_=ks_, kv_=kv_: y.wait_ge(ks_.h, kv_))
        Qst.dma(lambda y: y.dma_start(out=o_q, in_=s_qF[:].rearrange("p c t -> p (c t)")))
        for g in range(NKV):
            Qst.dma(lambda y, g=g: y.dma_start(out=o_k[g * 128:(g + 1) * 128, :],
                                               in_=s_kF[g][:]))
        Qst.dma(lambda y: y.dma_start(out=o_v, in_=s_vF[:].rearrange("p c t -> p (c t)")))
        Qst.dma(lambda y: y.dma_start(out=o_ln[:, 0:NCH], in_=s_lnm[:]))
        Qst.dma(lambda y: y.dma_start(out=o_ln[:, NCH:2 * NCH], in_=s_lnr[:]))
        Qst.dma(lambda y: y.dma_start(out=o_ln[:, 2 * NCH:3 * NCH], in_=s_lnv[:]))
        Qst.dma(lambda y: y.dma_start(out=o_ln[:, 3 * NCH:4 * NCH], in_=s_mx1[:]))

    if "C" not in stages:
        b.v(lambda v: v.drain())
        tz = m_T.n
        vz = m_V.n
        b.g(lambda g, tz=tz: g.wait_ge(m_T.h, tz))
        b.g(lambda g, vz=vz: g.wait_ge(m_V.h, vz))
        for tt in range(16):
            for q4 in range(4):
                Qg.dma(lambda g, tt=tt, q4=q4: g.dma_start(
                    out=d_out[tt * 128:(tt + 1) * 128, q4 * 512:(q4 + 1) * 512],
                    in_=s_attnT[:, 0, 0:512]))

    for Q in (Qy, Qg, Qs, Qst):
        Q.wait_all(b.y)

    b.flush()
    if pctx[0] is not None:
        pctx[0].close()
        pctx[0] = None
    for k in ("SC", "SB", "SA", "SQKV", "SBC"):
        scopes[k].close()
    ctx.close()
    return nc, d_dbg


def host_inputs(x_b, wq, wk, wv, wo, qnw, knw, w1, b1, w2, b2):
    asc = np.ascontiguousarray
    wqT = asc(wq.T)
    wkvT = asc(np.concatenate([wk, wv], axis=0).T)
    d = dict(
        x_tok=asc(x_b),
        wqT_blk=asc(wqT.reshape(D, 16, 128).transpose(1, 0, 2)).reshape(16 * D, 128),
        wkvT_blk=asc(wkvT.reshape(D, 8, 128).transpose(1, 0, 2)).reshape(8 * D, 128),
        woT=asc(wo.T),
        w1T=asc(w1.T),
        w2T=asc(w2.T),
        qnw_col=asc(qnw.reshape(NCH, 128).T),
        knw_col=asc(knw.reshape(4, 128).T),
        b1_row=asc(b1.reshape(1, INNER)),
        b2_row=asc(b2.reshape(1, D)),
    )
    d.update(_host_consts())
    return d


def kernel(L0_attn_x, L0_attn_wq, L0_attn_wk, L0_attn_wv, L0_attn_wo,
           L0_attn_q_norm_w, L0_attn_k_norm_w, L0_ffn_w1, L0_ffn_b1,
           L0_ffn_w2, L0_ffn_b2):
    nc, _ = build_program()
    in_maps = [host_inputs(L0_attn_x[c], L0_attn_wq, L0_attn_wk, L0_attn_wv,
                           L0_attn_wo, L0_attn_q_norm_w, L0_attn_k_norm_w,
                           L0_ffn_w1, L0_ffn_b1, L0_ffn_w2, L0_ffn_b2)
               for c in range(8)]
    res = run_bass_kernel_spmd(nc, in_maps, core_ids=list(range(8)))
    return np.stack([r["out"] for r in res.results], axis=0)
